# version 11 (final)
# speedup vs baseline: 1.1149x; 1.0060x over previous
import numpy as np
import ml_dtypes
from contextlib import ExitStack
import concourse.bass as bass
import concourse.mybir as mybir
from concourse.bass_utils import run_bass_kernel_spmd

F32 = mybir.dt.float32
BF16 = mybir.dt.bfloat16
AF = mybir.ActivationFunctionType
ALU = mybir.AluOpType
AX = mybir.AxisListType

NCORES = 8
B, S, D = 16, 2048, 1024
BL = B // NCORES
T = BL * S
NT = T // 128
NTS = S // 128


def _bucket_table():
    d = np.arange(-2047, 2048)
    nb = 16
    buckets = (d > 0).astype(np.int64) * nb
    rp = np.abs(d)
    max_exact = nb // 2
    is_small = rp < max_exact
    log_ratio = np.log(np.maximum(rp, 1).astype(np.float32) / max_exact) / np.log(128.0 / max_exact)
    large = max_exact + (log_ratio * (nb - max_exact)).astype(np.int64)
    large = np.minimum(large, nb - 1)
    return buckets + np.where(is_small, rp, large)

def build_constants():
    c = {}
    oh = np.zeros((32, 4224), np.float32)
    b = _bucket_table()
    oh[b, np.arange(4095)] = 1.0
    c["c_onehotT"] = oh
    ident = np.eye(128, dtype=np.float32)
    c["c_ident"] = ident
    p = np.arange(128)[:, None]
    f = np.arange(384)[None, :]
    band = (f >= p) & (f <= p + 256)
    NEG = np.float32(-3.0e38)
    for j, extra in ((0, f >= 256), (1, f >= 128), (2, f >= 0)):
        m = np.where(band & extra, np.float32(0.0), NEG)
        c[f"c_mask{j}"] = m.astype(np.float32)
    half = np.arange(0, 64, 2, dtype=np.float32)
    inv_freq = 1.0 / 10000.0 ** (half / 64)
    t = np.arange(S, dtype=np.float32)
    freqs = t[:, None] * inv_freq[None, :]
    freqs = np.concatenate([freqs, freqs], -1)
    base_scale = (half + 0.4 * 64) / (1.4 * 64)
    power = (t - S // 2) / (256 // 2)
    scale = base_scale[None, :] ** power[:, None]
    scale = np.concatenate([scale, scale], -1)
    cs, sn = np.cos(freqs), np.sin(freqs)
    def tile64(x):
        return np.ascontiguousarray(
            x.reshape(16, 128, 64).transpose(1, 0, 2).reshape(128, 16 * 64).astype(np.float32))
    c["c_csq"] = tile64(cs * scale)
    c["c_snq"] = tile64(sn * scale)
    c["c_csk"] = tile64(cs / scale)
    c["c_snk"] = tile64(sn / scale)
    return c

def permute_w1(W1, b1):
    L = W1.shape[0]
    W1P = np.zeros((L, 1024, 5632), np.float32)
    B1P = np.zeros((L, 128, 44), np.float32)
    a, g = W1[:, :, :2730], W1[:, :, 2730:]
    ba, bg = b1[:, :2730], b1[:, 2730:]
    ap = np.zeros((L, 1024, 2816), np.float32); ap[:, :, :2730] = a
    gp = np.zeros((L, 1024, 2816), np.float32); gp[:, :, :2730] = g
    bap = np.zeros((L, 2816), np.float32); bap[:, :2730] = ba
    bgp = np.zeros((L, 2816), np.float32); bgp[:, :2730] = bg
    for cidx in range(22):
        W1P[:, :, (2 * cidx) * 128:(2 * cidx + 1) * 128] = ap[:, :, cidx * 128:(cidx + 1) * 128]
        W1P[:, :, (2 * cidx + 1) * 128:(2 * cidx + 2) * 128] = gp[:, :, cidx * 128:(cidx + 1) * 128]
        B1P[:, :, 2 * cidx] = bap[:, cidx * 128:(cidx + 1) * 128]
        B1P[:, :, 2 * cidx + 1] = bgp[:, cidx * 128:(cidx + 1) * 128]
    return W1P, B1P

def pad_w2(W2):
    L = W2.shape[0]
    W2P = np.zeros((L, 2816, 1024), np.float32)
    W2P[:, :2730] = W2
    return W2P


DRAM_SPECS = {
    "x0": ((T, D), F32),
    "OUT": ((T, D), F32),
    "L0_attn_in_w": ((2048, 1024), F32),
    "L0_attn_in_b": ((2048,), F32),
    "L0_attn_conv_w": ((2048, 1, 4), F32),
    "L0_attn_conv_b": ((2048,), F32),
    "L0_attn_xp_w": ((32, 2048), F32),
    "L0_attn_xp_b": ((32,), F32),
    "L0_attn_out_w": ((1024, 16), F32),
    "L0_attn_out_b": ((1024,), F32),
    "L0_ffn_router_w": ((8, 1024), F32),
    "L0_ffn_w13": ((8, 1024, 4096), F32),
    "L0_ffn_w2": ((8, 2048, 1024), F32),
    "L1_attn_Wq": ((1024, 1024), F32),
    "L1_attn_Wk": ((1024, 1024), F32),
    "L1_attn_Wv": ((1024, 1024), F32),
    "L1_attn_Wo": ((1024, 1024), F32),
    "L1_attn_rel_bias": ((32, 16), F32),
    "L1_ffn_Wqkv": ((6, 1024, 3072), F32),
    "L1_ffn_q_scale": ((6, 64), F32),
    "L1_ffn_k_scale": ((6, 64), F32),
    "L1_ffn_Wo": ((6, 1024, 1024), F32),
    "L1_ffn_ln_g": ((6, 1024), F32),
    "L1_ffn_ln_b": ((6, 1024), F32),
    "W1P": ((6, 1024, 5632), F32),
    "B1P": ((6, 128, 44), F32),
    "W2P": ((6, 2816, 1024), F32),
    "L1_ffn_b2": ((6, 1024), F32),
    "c_onehotT": ((32, 4224), F32),
    "c_ident": ((128, 128), F32),
    "c_mask0": ((128, 384), F32),
    "c_mask1": ((128, 384), F32),
    "c_mask2": ((128, 384), F32),
    "c_csq": ((128, 1024), F32),
    "c_snq": ((128, 1024), F32),
    "c_csk": ((128, 1024), F32),
    "c_snk": ((128, 1024), F32),
    "xnbf": ((T, D), BF16),
    "xnf32": ((T, D), F32),
    "aout": ((T, D), F32),
    "fout": ((T, D), F32),
    "x1": ((T, D), F32),
    "gdram": ((T, 8), F32),
    "ebdP": ((16, 4352), F32),
    "STC": ((128, 16, 4224), F32),
    "qfm": ((1024, T), BF16),
    "kfm": ((1024, T), BF16),
    "vtm": ((T, D), BF16),
    "xi": ((T, D), F32),
    "xsbf": ((T, D), BF16),
    "qtm": ((T, D), BF16),
    "ktm": ((T, D), BF16),
    "dbg": ((T, 32), F32),
    "ctxfm": ((1024, T), BF16),
}


class Sem:
    def __init__(self, handle):
        self.h = handle
        self.count = 0

    def inc(self, inst, n):
        inst.then_inc(self.h, n)
        self.count += n
        return self.count

    def wait(self, e, at=None):
        e.wait_ge(self.h, self.count if at is None else at)


class P:

    def __init__(self, inputs, outputs, scratch):
        self.nc = bass.Bass("TRN2", target_bir_lowering=False)
        self.dram = {}
        for name in inputs:
            shape, dt = DRAM_SPECS[name]
            self.dram[name] = self.nc.dram_tensor(name, list(shape), dt, kind="ExternalInput")
        for name in outputs:
            shape, dt = DRAM_SPECS[name]
            self.dram[name] = self.nc.dram_tensor(name, list(shape), dt, kind="ExternalOutput")
        for name in scratch:
            shape, dt = DRAM_SPECS[name]
            self.dram[name] = self.nc.dram_tensor(name, list(shape), dt, kind="Internal")
        self.stack = ExitStack()
        self.sd = Sem(self.stack.enter_context(self.nc.semaphore("sem_d")))
        self.sg = Sem(self.stack.enter_context(self.nc.semaphore("sem_g")))
        self.ss = Sem(self.stack.enter_context(self.nc.semaphore("sem_s")))
        self.c1 = Sem(self.stack.enter_context(self.nc.semaphore("sem_c1")))
        self.c2 = Sem(self.stack.enter_context(self.nc.semaphore("sem_c2")))
        self.c3 = Sem(self.stack.enter_context(self.nc.semaphore("sem_c3")))
        self.c4 = Sem(self.stack.enter_context(self.nc.semaphore("sem_c4")))
        self.fv = Sem(self.stack.enter_context(self.nc.semaphore("sem_fv")))
        self.fs = Sem(self.stack.enter_context(self.nc.semaphore("sem_fs")))
        self.fp = Sem(self.stack.enter_context(self.nc.semaphore("sem_fp")))
        self.e = [Sem(self.stack.enter_context(self.nc.semaphore(f"sem_e{i}")))
                  for i in range(8)]

    def d(self, name):
        return self.dram[name].ap()

    _uid = [0]

    def uname(self, n):
        P._uid[0] += 1
        return f"{n}_u{P._uid[0]}"


    def sb(self, name, shape, dt=F32):
        return self.stack.enter_context(self.nc.sbuf_tensor(self.uname(name), list(shape), dt))

    def ps(self, name, shape, dt=F32):
        return self.stack.enter_context(self.nc.psum_tensor(self.uname(name), list(shape), dt))

    def close(self):
        self.stack.close()


def dma(P_, e, dst, src, sem=None, **kw):
    sem = sem or P_.sd
    inst = e.dma_start(dst, src, **kw)
    sem.inc(inst, 16)
    return sem.count


def FV(P_, e, inst):
    P_.fv.inc(inst, 1)
    e.wait_ge(P_.fv.h, P_.fv.count)

def FS(P_, e, inst):
    P_.fs.inc(inst, 1)
    e.wait_ge(P_.fs.h, P_.fs.count)

def FP(P_, e, inst):
    P_.fp.inc(inst, 1)
    e.wait_ge(P_.fp.h, P_.fp.count)


def ldT(P_, e, dst_sb, src_dram, rows, cols, dst_col0=0, sem=None):
    for c in range(cols // 128):
        for tt in range(rows // 128):
            dma(P_, e,
                dst_sb[:, dst_col0 + c * rows + tt * 128: dst_col0 + c * rows + tt * 128 + 128],
                src_dram[tt * 128:(tt + 1) * 128, c * 128:(c + 1) * 128],
                sem=sem, transpose=True)



def emit_ln_simple(P_, src, want_f32, group=8, dst_bf='xnbf', dst_f32='xnf32'):
    nc = P_.nc
    with ExitStack() as st:
        xb = st.enter_context(nc.sbuf_tensor(P_.uname("ln_x"), [128, group * D], F32))
        ob = st.enter_context(nc.sbuf_tensor(P_.uname("ln_o"), [128, group * D], BF16))
        ofb = st.enter_context(nc.sbuf_tensor(P_.uname("ln_of"), [128, group * D], F32))
        scr = st.enter_context(nc.sbuf_tensor(P_.uname("ln_scr"), [128, D], F32))
        stt = st.enter_context(nc.sbuf_tensor(P_.uname("ln_st"), [128, 8 * group], F32))
        for g0 in range(0, NT, group):
            n = min(group, NT - g0)
            with nc.Block() as blk:
                @blk.sync
                def _(e, g0=g0, n=n):
                    for j in range(n):
                        i = g0 + j
                        dma(P_, e, xb[:, j * D:(j + 1) * D], P_.d(src)[i * 128:(i + 1) * 128, :])
                    P_.sd.wait(e)
            with nc.Block() as blk:
                @blk.vector
                def _(e, n=n):
                    for j in range(n):
                        x = xb[:, j * D:(j + 1) * D]
                        sl = stt[:, j * 8:(j + 1) * 8]
                        FV(P_, e, e.reduce_sum(sl[:, 0:1], x, AX.X))
                        e.tensor_scalar_mul(sl[:, 1:2], sl[:, 0:1], -1.0 / D)
            with nc.Block() as blk:
                @blk.scalar
                def _(e, n=n):
                    for j in range(n):
                        x = xb[:, j * D:(j + 1) * D]
                        sl = stt[:, j * 8:(j + 1) * 8]
                        e.activation(scr[:, :], x, AF.Square, bias=sl[:, 1:2], scale=1.0,
                                     accum_out=sl[:, 2:3])
            with nc.Block() as blk:
                @blk.vector
                def _(e, n=n):
                    for j in range(n):
                        sl = stt[:, j * 8:(j + 1) * 8]
                        e.tensor_scalar(sl[:, 3:4], sl[:, 2:3], 1.0 / D, 1e-5, ALU.mult, ALU.add)
            with nc.Block() as blk:
                @blk.scalar
                def _(e, n=n):
                    for j in range(n):
                        sl = stt[:, j * 8:(j + 1) * 8]
                        e.activation(sl[:, 4:5], sl[:, 3:4], AF.Sqrt)
            with nc.Block() as blk:
                @blk.vector
                def _(e, n=n):
                    for j in range(n):
                        sl = stt[:, j * 8:(j + 1) * 8]
                        FV(P_, e, e.reciprocal(sl[:, 5:6], sl[:, 4:5]))
                        e.tensor_mul(sl[:, 6:7], sl[:, 1:2], sl[:, 5:6])
            with nc.Block() as blk:
                @blk.scalar
                def _(e, n=n):
                    for j in range(n):
                        x = xb[:, j * D:(j + 1) * D]
                        sl = stt[:, j * 8:(j + 1) * 8]
                        e.activation(ob[:, j * D:(j + 1) * D], x, AF.Identity,
                                     bias=sl[:, 6:7], scale=sl[:, 5:6])
                        if want_f32:
                            e.activation(ofb[:, j * D:(j + 1) * D], x, AF.Identity,
                                         bias=sl[:, 6:7], scale=sl[:, 5:6])
            with nc.Block() as blk:
                @blk.sync
                def _(e, g0=g0, n=n):
                    for j in range(n):
                        i = g0 + j
                        dma(P_, e, P_.d(dst_bf)[i * 128:(i + 1) * 128, :], ob[:, j * D:(j + 1) * D])
                        if want_f32:
                            dma(P_, e, P_.d(dst_f32)[i * 128:(i + 1) * 128, :],
                                ofb[:, j * D:(j + 1) * D])
                    P_.sd.wait(e)

def col_ap(P_, name, off, n):
    return bass.AP(P_.dram[name], off, [[1, n], [1, 1]])



def emit_l0_attn(P_):
    nc = P_.nc
    DI = 2048
    with ExitStack() as st:
        sb = lambda n, s, d=F32: st.enter_context(nc.sbuf_tensor(P_.uname(n), list(s), d))
        inwT = sb("inwT", (128, 8 * DI), BF16)
        ident_bf = sb("identbf", (128, 128), BF16)
        inb = sb("inb", (128, 16), F32)
        cw = sb("cw", (128, 16 * 4), F32)
        cb = sb("cb", (128, 16), F32)
        xpw65 = sb("xpw65", (128, 16 * 65), BF16)
        xpb = sb("xpb", (128, 1), F32)
        owb = sb("owb", (64, 1024), BF16)
        ones_bf = sb("ones_bf", (1, 16), BF16)

        with ExitStack() as st2:
            stg = st2.enter_context(nc.sbuf_tensor(P_.uname("stg"), [128, 16 * 1024], F32))
            stgb = st2.enter_context(nc.sbuf_tensor(P_.uname("stgb"), [128, 16 * 1024], BF16))
            ps_t = st2.enter_context(nc.psum_tensor(P_.uname("ps_t"), [128, 4096], BF16))
            with nc.Block() as blk:
                @blk.sync
                def _(e):
                    dma(P_, e, stg[:, 0:128], P_.d("c_ident"))
                    P_.sd.wait(e)
            with nc.Block() as blk:
                @blk.vector
                def _(e):
                    e.tensor_copy(ident_bf[:, :], stg[:, 0:128])
            with nc.Block() as blk:
                @blk.sync
                def _(e):
                    dma(P_, e, stg[:, :].rearrange("p (r c) -> p r c", r=16),
                        P_.d("L0_attn_in_w").rearrange("(r p) c -> p r c", p=128))
                    for c in range(16):
                        dma(P_, e, inb[:, c:c + 1], col_ap(P_, "L0_attn_in_b", c * 128, 128))
                        dma(P_, e, cb[:, c:c + 1], col_ap(P_, "L0_attn_conv_b", c * 128, 128))
                        dma(P_, e, cw[:, c * 4:(c + 1) * 4],
                            P_.d("L0_attn_conv_w").rearrange("(r p) a k -> r p (a k)", p=128)[c])
                    dma(P_, e, xpb[0:16, :], col_ap(P_, "L0_attn_xp_b", 0, 16))
                    dma(P_, e, xpb[32:48, :], col_ap(P_, "L0_attn_xp_b", 16, 16))
                    P_.sd.wait(e)
            with nc.Block() as blk:
                @blk.vector
                def _(e):
                    e.tensor_copy(stgb[:, :], stg[:, :])
            for r0 in range(0, 16, 2):
                with nc.Block() as blk:
                    @blk.tensor
                    def _(e, r0=r0):
                        for rr in range(2):
                            for c in range(8):
                                e.transpose(ps_t[:, (rr * 8 + c) * 128:(rr * 8 + c + 1) * 128],
                                            stgb[:, (r0 + rr) * 1024 + c * 128:(r0 + rr) * 1024 + (c + 1) * 128],
                                            ident_bf[:, :])
                with nc.Block() as blk:
                    @blk.vector
                    def _(e, r0=r0):
                        for rr in range(2):
                            for c in range(8):
                                e.tensor_copy(
                                    inwT[:, c * DI + (r0 + rr) * 128: c * DI + (r0 + rr) * 128 + 128],
                                    ps_t[:, (rr * 8 + c) * 128:(rr * 8 + c + 1) * 128])
            with nc.Block() as blk:
                @blk.sync
                def _(e):
                    dma(P_, e, stg[0:32, 0:2048], P_.d("L0_attn_xp_w"))
                    dma(P_, e, stg[:, 2048:2048 + 128].rearrange("p (r c) -> p r c", r=8),
                        P_.d("L0_attn_out_w").rearrange("(r p) c -> p r c", p=128))
                    dma(P_, e, stg[0:1, 4096:4096 + 1024],
                        P_.d("L0_attn_out_b").rearrange("(a c) -> a c", a=1))
                    P_.sd.wait(e)
            with nc.Block() as blk:
                @blk.vector
                def _(e):
                    e.memset(xpw65[:, :], 0.0)
                    e.memset(ones_bf[:, :], 1.0)
                    e.tensor_copy(stgb[0:32, 0:2048], stg[0:32, 0:2048])
                    e.tensor_copy(stgb[:, 2048:2048 + 128], stg[:, 2048:2048 + 128])
                    e.tensor_copy(stgb[0:1, 4096:4096 + 1024], stg[0:1, 4096:4096 + 1024])
            with nc.Block() as blk:
                @blk.tensor
                def _(e):
                    for c in range(16):
                        e.transpose(ps_t[:, c * 32:(c + 1) * 32],
                                    stgb[0:32, c * 128:(c + 1) * 128], ident_bf[0:32, 0:32])
                    for c in range(8):
                        e.transpose(ps_t[0:16, 2048 + c * 128:2048 + (c + 1) * 128],
                                    stgb[:, 2048 + c * 16:2048 + (c + 1) * 16], ident_bf[:, :])
            with nc.Block() as blk:
                @blk.vector
                def _(e):
                    for c in range(16):
                        e.tensor_copy(xpw65[:, c * 65 + 0: c * 65 + 16], ps_t[:, c * 32: c * 32 + 16])
                        e.tensor_copy(xpw65[:, c * 65 + 32: c * 65 + 48], ps_t[:, c * 32 + 16: c * 32 + 32])
                        e.memset(xpw65[:, c * 65 + 64: c * 65 + 65], 1.0)
                    e.tensor_copy(owb[0:16, :], ps_t[0:16, 2048:2048 + 1024])
            with nc.Block() as blk:
                @blk.sync
                def _(e):
                    dma(P_, e, owb[16:17, :], stgb[0:1, 4096:4096 + 1024])
                    dma(P_, e, owb[48:49, :], stgb[0:1, 4096:4096 + 1024])
                    dma(P_, e, owb[32:48, :], owb[0:16, :])
                    P_.sd.wait(e)

        xnT = sb("xnT", (128, 8 * S), BF16)
        hbf = sb("hbf", (128, 16 * S), BF16)
        strip = sb("strip", (128, 2051), F32)
        acc = sb("acc", (128, 2048), F32)
        abuf = sb("abuf", (64, S), F32)
        bbuf = sb("bbuf", (64, S), F32)
        ysf = sb("ysf", (64, S), F32)
        ysbf = sb("ysbf", (64, S), BF16)
        tmp16 = sb("tmp16", (16, 512), F32)
        hsrow = sb("hsrow", (1, 512), BF16)
        aosb = sb("aosb", (128, 2 * 1024), F32)
        with nc.Block() as blk:
            @blk.vector
            def _(e):
                e.memset(ysbf[:, :], 1.0)

        for s in range(BL):
            toff = s * S
            with nc.Block() as blk:
                @blk.sync
                def _(e, toff=toff):
                    ldT(P_, e, xnT, P_.d("xnbf")[toff:toff + S, :], S, 1024)
                    P_.sd.wait(e)
            with ExitStack() as st2:
                ps_h = st2.enter_context(nc.psum_tensor(P_.uname("ps_h"), [128, 2048], F32))
                for ch in range(16):
                    with nc.Block() as blk:
                        @blk.tensor
                        def _(e, ch=ch):
                            for tc in range(4):
                                for kc in range(8):
                                    e.matmul(ps_h[:, tc * 512:(tc + 1) * 512],
                                             inwT[:, kc * DI + ch * 128: kc * DI + (ch + 1) * 128],
                                             xnT[:, kc * S + tc * 512: kc * S + (tc + 1) * 512],
                                             start=(kc == 0), stop=(kc == 7))
                    with nc.Block() as blk:
                        @blk.scalar
                        def _(e, ch=ch):
                            e.activation(strip[:, 3:2051], ps_h[:, :],
                                         AF.Identity, bias=inb[:, ch:ch + 1], scale=1.0)
                        @blk.vector
                        def _(e):
                            e.memset(strip[:, 0:3], 0.0)
                    with nc.Block() as blk:
                        @blk.vector
                        def _(e, ch=ch):
                            FV(P_, e, e.tensor_scalar_mul(acc[:, :], strip[:, 0:2048],
                                                           cw[:, ch * 4: ch * 4 + 1]))
                            for k in range(1, 4):
                                FV(P_, e, e.scalar_tensor_tensor(acc[:, :], strip[:, k: k + 2048],
                                                       cw[:, ch * 4 + k: ch * 4 + k + 1],
                                                       acc[:, :], ALU.mult, ALU.add))
                    with nc.Block() as blk:
                        @blk.scalar
                        def _(e, ch=ch):
                            e.activation(hbf[:, ch * S:(ch + 1) * S], acc[:, :],
                                         AF.Silu, bias=cb[:, ch:ch + 1], scale=1.0)
            with ExitStack() as st2:
                ps65 = st2.enter_context(nc.psum_tensor(P_.uname("ps65"), [128, 512], F32))
                ps_hs = st2.enter_context(nc.psum_tensor(P_.uname("ps_hs"), [16, 512], F32))
                for tc in range(4):
                    with nc.Block() as blk:
                        @blk.tensor
                        def _(e, tc=tc):
                            for ch in range(16):
                                e.matmul(ps65[0:65, :], xpw65[:, ch * 65:(ch + 1) * 65],
                                         hbf[:, ch * S + tc * 512: ch * S + (tc + 1) * 512],
                                         start=(ch == 0), stop=(ch == 15))
                    with nc.Block() as blk:
                        @blk.scalar
                        def _(e):
                            e.activation(tmp16[:, :], ps65[0:16, :], AF.Exp, bias=xpb[0:16, :], scale=1.0)
                    with nc.Block() as blk:
                        @blk.scalar
                        def _(e, tc=tc, s=s):
                            e.activation(abuf[32 * s:32 * s + 16, tc * 512:(tc + 1) * 512],
                                         tmp16[:, :], AF.Exp, scale=-1.0)
                            e.activation(hsrow[:, :], ps65[64:65, :], AF.Identity)
                    with nc.Block() as blk:
                        @blk.scalar
                        def _(e):
                            e.activation(tmp16[:, :], ps65[32:48, :], AF.Silu, bias=xpb[32:48, :], scale=1.0)
                        @blk.tensor
                        def _(e):
                            e.matmul(ps_hs[:, :], ones_bf[:, :], hsrow[:, :], start=True, stop=True)
                    with nc.Block() as blk:
                        @blk.vector
                        def _(e, tc=tc, s=s):
                            e.tensor_tensor(bbuf[32 * s:32 * s + 16, tc * 512:(tc + 1) * 512],
                                            tmp16[:, :], ps_hs[:, :], ALU.mult)
            with nc.Block() as blk:
                @blk.vector
                def _(e, s=s):
                    e.tensor_tensor_scan(ysf[32 * s:32 * s + 16, :], abuf[32 * s:32 * s + 16, :],
                                         bbuf[32 * s:32 * s + 16, :], 0.0, ALU.mult, ALU.add)
            with nc.Block() as blk:
                @blk.scalar
                def _(e, s=s):
                    e.activation(ysbf[32 * s:32 * s + 16, :], ysf[32 * s:32 * s + 16, :], AF.Identity)
            with ExitStack() as st2:
                ps_o = st2.enter_context(nc.psum_tensor(P_.uname("ps_o"), [128, 2 * 1024], F32))
                for g0 in range(0, NTS, 2):
                    with nc.Block() as blk:
                        @blk.tensor
                        def _(e, g0=g0, s=s):
                            for j in range(2):
                                tt = g0 + j
                                for nh in range(2):
                                    e.matmul(ps_o[:, j * 1024 + nh * 512: j * 1024 + (nh + 1) * 512],
                                             ysbf[32 * s:32 * s + 17, tt * 128:(tt + 1) * 128],
                                             owb[32 * s:32 * s + 17, nh * 512:(nh + 1) * 512],
                                             start=True, stop=True)
                    with nc.Block() as blk:
                        @blk.vector
                        def _(e):
                            e.tensor_copy(aosb[:, :], ps_o[:, :])
                    with nc.Block() as blk:
                        @blk.sync
                        def _(e, g0=g0, toff=toff):
                            for j in range(2):
                                tt = g0 + j
                                dma(P_, e, P_.d("aout")[toff + tt * 128: toff + (tt + 1) * 128, :],
                                    aosb[:, j * 1024:(j + 1) * 1024])
                            P_.sd.wait(e)


def emit_moe_gate(P_):
    nc = P_.nc
    with ExitStack() as st:
        sb = lambda n, s, d=F32: st.enter_context(nc.sbuf_tensor(P_.uname(n), list(s), d))
        xnT = sb("gxnT", (128, 8 * T), BF16)
        rwT = sb("rwT", (128, 8 * 8), BF16)
        ident_bf = sb("gident", (128, 128), BF16)
        stg = sb("gstg", (128, 1024), F32)
        stgb = sb("gstgb", (128, 1024), BF16)
        gsb = sb("gsb", (128, 8 * 16), F32)
        prb = sb("prb", (128, 16 * 8), F32)
        ps_g = st.enter_context(nc.psum_tensor(P_.uname("ps_g"), [128, 4 * 512], F32))
        ps_t = st.enter_context(nc.psum_tensor(P_.uname("ps_gt"), [128, 8], BF16))
        with nc.Block() as blk:
            @blk.sync
            def _(e):
                dma(P_, e, stg[:, 896:1024], P_.d("c_ident"))
                P_.sd.wait(e)
        with nc.Block() as blk:
            @blk.vector
            def _(e):
                e.tensor_copy(ident_bf[:, :], stg[:, 896:1024])
        with nc.Block() as blk:
            @blk.sync
            def _(e):
                ldT(P_, e, xnT, P_.d("xnbf"), T, 1024)
                dma(P_, e, stg[0:8, :], P_.d("L0_ffn_router_w"))
                P_.sd.wait(e)
        with nc.Block() as blk:
            @blk.vector
            def _(e):
                e.tensor_copy(stgb[0:8, :], stg[0:8, :])
        for c in range(8):
            with nc.Block() as blk:
                @blk.tensor
                def _(e, c=c):
                    e.transpose(ps_t[:, :], stgb[0:8, c * 128:(c + 1) * 128], ident_bf[0:8, 0:8])
            with nc.Block() as blk:
                @blk.vector
                def _(e, c=c):
                    e.tensor_copy(rwT[:, c * 8:(c + 1) * 8], ps_t[:, :])
        for g0 in range(0, NT, 4):
            with nc.Block() as blk:
                @blk.tensor
                def _(e, g0=g0):
                    for j in range(4):
                        tt = g0 + j
                        for kc in range(8):
                            e.matmul(ps_g[:, j * 512: j * 512 + 8],
                                     xnT[:, kc * T + tt * 128: kc * T + (tt + 1) * 128],
                                     rwT[:, kc * 8:(kc + 1) * 8],
                                     start=(kc == 0), stop=(kc == 7))
            with nc.Block() as blk:
                @blk.vector
                def _(e, g0=g0):
                    for j in range(4):
                        sl = gsb[:, j * 16:(j + 1) * 16]
                        FV(P_, e, e.reduce_max(sl[:, 8:9], ps_g[:, j * 512: j * 512 + 8], AX.X, negate=True))
                        e.tensor_copy(prb[:, 32 + j * 8: 32 + (j + 1) * 8], ps_g[:, j * 512: j * 512 + 8])
            with nc.Block() as blk:
                @blk.scalar
                def _(e, g0=g0):
                    for j in range(4):
                        sl = gsb[:, j * 16:(j + 1) * 16]
                        e.activation(prb[:, j * 8:(j + 1) * 8], ps_g[:, j * 512: j * 512 + 8],
                                     AF.Exp, bias=sl[:, 8:9], scale=1.0, accum_out=sl[:, 9:10])
            with nc.Block() as blk:
                @blk.vector
                def _(e, g0=g0):
                    for j in range(4):
                        sl = gsb[:, j * 16:(j + 1) * 16]
                        pr = prb[:, j * 8:(j + 1) * 8]
                        FV(P_, e, e.reciprocal(sl[:, 10:11], sl[:, 9:10]))
                        FV(P_, e, e.tensor_scalar_mul(pr, pr, sl[:, 10:11]))
                        FV(P_, e, e.reduce_max(sl[:, 11:12], pr, AX.X))
                        FV(P_, e, e.tensor_scalar(sl[:, 0:8], pr, sl[:, 11:12], None, ALU.is_equal))
                        FV(P_, e, e.scalar_tensor_tensor(sl[:, 0:8], sl[:, 0:8], -1e30, pr,
                                                         ALU.mult, ALU.add))
                        FV(P_, e, e.reduce_max(sl[:, 12:13], sl[:, 0:8], AX.X))
                        FV(P_, e, e.tensor_scalar(sl[:, 0:8], pr, sl[:, 12:13], None, ALU.is_ge))
                        FV(P_, e, e.tensor_tensor(sl[:, 0:8], sl[:, 0:8], pr, ALU.mult))
                        FV(P_, e, e.tensor_tensor(sl[:, 13:14], sl[:, 11:12], sl[:, 12:13], ALU.add))
                        FV(P_, e, e.tensor_scalar_add(sl[:, 13:14], sl[:, 13:14], 1e-6))
                        FV(P_, e, e.reciprocal(sl[:, 14:15], sl[:, 13:14]))
                        FV(P_, e, e.tensor_scalar_mul(sl[:, 0:8], sl[:, 0:8], sl[:, 14:15]))
            with nc.Block() as blk:
                @blk.sync
                def _(e, g0=g0):
                    for j in range(4):
                        tt = g0 + j
                        dma(P_, e, P_.d("gdram")[tt * 128:(tt + 1) * 128, :],
                            gsb[:, j * 16: j * 16 + 8])
                        if "dbg" in P_.dram:
                            dma(P_, e, P_.d("dbg")[tt * 128:(tt + 1) * 128, 0:16],
                                gsb[:, j * 16:(j + 1) * 16])
                            dma(P_, e, P_.d("dbg")[tt * 128:(tt + 1) * 128, 16:24],
                                prb[:, j * 8:(j + 1) * 8])
                            dma(P_, e, P_.d("dbg")[tt * 128:(tt + 1) * 128, 24:32],
                                prb[:, 32 + j * 8: 32 + (j + 1) * 8])
                    P_.sd.wait(e)


def emit_moe(P_):
    nc = P_.nc
    with ExitStack() as st:
        sb = lambda n, s, d=F32: st.enter_context(nc.sbuf_tensor(P_.uname(n), list(s), d))
        w13b = sb("w13b", (128, 8 * 4096), BF16)
        w2b = sb("w2b", (128, 16 * 1024), BF16)
        stg = sb("mstg", (128, 8 * 1024), F32)
        hbuf = sb("hbuf", (128, 16 * 1024), BF16)
        xmov = sb("xmov", (128, 8 * 1024), BF16)
        ysc = sb("ysc", (128, 2 * 1024), F32)
        g_sb = sb("g_sb", (128, NT * 8), F32)
        sgl = sb("sgl", (128, 2 * 512), F32)
        ps_a = st.enter_context(nc.psum_tensor(P_.uname("ps_ma"), [128, 2 * 512], F32))
        ps_b = st.enter_context(nc.psum_tensor(P_.uname("ps_mb"), [128, 2 * 512], F32))
        ps_y = st.enter_context(nc.psum_tensor(P_.uname("ps_my"), [128, 2 * 1024], F32))
        with nc.Block() as blk:
            @blk.sync
            def _(e):
                dma(P_, e, g_sb[:, :].rearrange("p (t g) -> p t g", g=8),
                    P_.d("gdram").rearrange("(t p) g -> p t g", p=128))
                P_.sd.wait(e)
        for e8 in range(8):
            with nc.Block() as blk:
                @blk.gpsimd
                def _(e, e8=e8):
                    for part in range(4):
                        dma(P_, e,
                            w13b[:, :].rearrange("p (k c) -> p k c", k=8)
                            [:, part * 2:(part + 1) * 2, :],
                            P_.d("L0_ffn_w13")[e8].rearrange("(k p) c -> p k c", p=128)
                            [:, part * 2:(part + 1) * 2, :], sem=P_.sg)
                        dma(P_, e,
                            w2b[:, :].rearrange("p (k c) -> p k c", k=16)
                            [:, part * 4:(part + 1) * 4, :],
                            P_.d("L0_ffn_w2")[e8].rearrange("(k p) c -> p k c", p=128)
                            [:, part * 4:(part + 1) * 4, :], sem=P_.sg)
                    P_.sg.wait(e)
            for tg in range(4):
                with nc.Block() as blk:
                    @blk.sync
                    def _(e, tg=tg):
                        ldT(P_, e, xmov, P_.d("xnbf")[tg * 1024:(tg + 1) * 1024, :], 1024, 1024)
                        P_.sd.wait(e)
                c1b, c3b, c2b = P_.c1.count, P_.c3.count, P_.c2.count
                with nc.Block() as blk:
                    @blk.tensor
                    def _(e):
                        for u in range(32):
                            ich, ts = u // 2, u % 2
                            pr = u % 2
                            if u >= 2:
                                e.wait_ge(P_.c2.h, c2b + (u - 1))
                            for kc in range(8):
                                e.matmul(ps_a[:, pr * 512:(pr + 1) * 512],
                                         w13b[:, kc * 4096 + ich * 128: kc * 4096 + ich * 128 + 128],
                                         xmov[:, kc * 1024 + ts * 512: kc * 1024 + (ts + 1) * 512],
                                         start=(kc == 0), stop=(kc == 7))
                            for kc in range(8):
                                mm = e.matmul(ps_b[:, pr * 512:(pr + 1) * 512],
                                              w13b[:, kc * 4096 + 2048 + ich * 128: kc * 4096 + 2048 + ich * 128 + 128],
                                              xmov[:, kc * 1024 + ts * 512: kc * 1024 + (ts + 1) * 512],
                                              start=(kc == 0), stop=(kc == 7))
                            mm.then_inc(P_.c1.h, 1)
                        P_.c1.count += 32
                    @blk.scalar
                    def _(e):
                        for u in range(32):
                            pr = u % 2
                            e.wait_ge(P_.c1.h, c1b + u + 1)
                            act = e.activation(sgl[:, pr * 512:(pr + 1) * 512],
                                               ps_a[:, pr * 512:(pr + 1) * 512], AF.Silu)
                            act.then_inc(P_.c3.h, 1)
                        P_.c3.count += 32
                    @blk.vector
                    def _(e):
                        for u in range(32):
                            ich, ts = u // 2, u % 2
                            pr = u % 2
                            e.wait_ge(P_.c3.h, c3b + u + 1)
                            tt_i = e.tensor_tensor(
                                hbuf[:, ich * 1024 + ts * 512: ich * 1024 + (ts + 1) * 512],
                                sgl[:, pr * 512:(pr + 1) * 512],
                                ps_b[:, pr * 512:(pr + 1) * 512], ALU.mult)
                            tt_i.then_inc(P_.c2.h, 1)
                        P_.c2.count += 32
                c1b2, c4b, sgb = P_.c1.count, P_.c4.count, P_.sg.count
                with nc.Block() as blk:
                    @blk.tensor
                    def _(e):
                        for tt in range(8):
                            sl = tt % 2
                            if tt >= 2:
                                e.wait_ge(P_.c4.h, c4b + (tt - 1))
                            for ich in range(16):
                                for dh in range(2):
                                    mm = e.matmul(ps_y[:, sl * 1024 + dh * 512: sl * 1024 + (dh + 1) * 512],
                                                  hbuf[:, ich * 1024 + tt * 128: ich * 1024 + tt * 128 + 128],
                                                  w2b[:, ich * 1024 + dh * 512: ich * 1024 + (dh + 1) * 512],
                                                  start=(ich == 0), stop=(ich == 15))
                            mm.then_inc(P_.c1.h, 1)
                        P_.c1.count += 8
                    @blk.vector
                    def _(e, tg=tg, e8=e8):
                        for tt in range(8):
                            sl = tt % 2
                            e.wait_ge(P_.c1.h, c1b2 + tt + 1)
                            if tt >= 2:
                                e.wait_ge(P_.sg.h, sgb + 16 * (tt - 1))
                            gcol = g_sb[:, (tg * 8 + tt) * 8 + e8: (tg * 8 + tt) * 8 + e8 + 1]
                            ts_i = e.tensor_scalar_mul(ysc[:, sl * 1024:(sl + 1) * 1024],
                                                       ps_y[:, sl * 1024:(sl + 1) * 1024], gcol)
                            ts_i.then_inc(P_.c4.h, 1)
                        P_.c4.count += 8
                    @blk.gpsimd
                    def _(e, tg=tg, e8=e8):
                        for tt in range(8):
                            sl = tt % 2
                            e.wait_ge(P_.c4.h, c4b + tt + 1)
                            row0 = tg * 1024 + tt * 128
                            kw = {} if e8 == 0 else {"accum_op": ALU.add}
                            dma(P_, e, P_.d("fout")[row0:row0 + 128, :],
                                ysc[:, sl * 1024:(sl + 1) * 1024], sem=P_.sg, **kw)
                        P_.sg.wait(e)


def emit_l1_qkv(P_):
    nc = P_.nc
    with ExitStack() as st:
        sb = lambda n, s, d=F32: st.enter_context(nc.sbuf_tensor(P_.uname(n), list(s), d))
        xnT = sb("qxnT", (128, 8 * T), BF16)
        wb = sb("qwb", (128, 8 * 1024), BF16)
        ev = sb("qev", (128, 2048), BF16)
        ps = st.enter_context(nc.psum_tensor(P_.uname("ps_q"), [128, 2048], F32))
        with nc.Block() as blk:
            @blk.sync
            def _(e):
                ldT(P_, e, xnT, P_.d("xnbf"), T, 1024)
                P_.sd.wait(e)
        for wname, dst, fm in (("L1_attn_Wq", "qfm", True), ("L1_attn_Wk", "kfm", True),
                               ("L1_attn_Wv", "vtm", False)):
            with nc.Block() as blk:
                @blk.gpsimd
                def _(e, wname=wname):
                    for part in range(4):
                        dma(P_, e,
                            wb[:, :].rearrange("p (k c) -> p k c", k=8)
                            [:, part * 2:(part + 1) * 2, :],
                            P_.d(wname).rearrange("(k p) c -> p k c", p=128)
                            [:, part * 2:(part + 1) * 2, :], sem=P_.sg)
                    P_.sg.wait(e)
            if fm:
                for mc in range(8):
                    for tg in range(0, T, 2048):
                        with nc.Block() as blk:
                            @blk.tensor
                            def _(e, mc=mc, tg=tg):
                                for tc in range(4):
                                    for kc in range(8):
                                        e.matmul(ps[:, tc * 512:(tc + 1) * 512],
                                                 wb[:, kc * 1024 + mc * 128: kc * 1024 + mc * 128 + 128],
                                                 xnT[:, kc * T + tg + tc * 512: kc * T + tg + (tc + 1) * 512],
                                                 start=(kc == 0), stop=(kc == 7))
                        with nc.Block() as blk:
                            @blk.vector
                            def _(e):
                                e.tensor_copy(ev[:, :], ps[:, :])
                        with nc.Block() as blk:
                            @blk.sync
                            def _(e, mc=mc, tg=tg, dst=dst):
                                dma(P_, e, P_.d(dst)[mc * 128:(mc + 1) * 128, tg:tg + 2048], ev[:, :])
                                P_.sd.wait(e)
            else:
                for g0 in range(0, NT, 2):
                    with nc.Block() as blk:
                        @blk.tensor
                        def _(e, g0=g0):
                            for j in range(2):
                                tt = g0 + j
                                for nh in range(2):
                                    for kc in range(8):
                                        e.matmul(ps[:, j * 1024 + nh * 512: j * 1024 + (nh + 1) * 512],
                                                 xnT[:, kc * T + tt * 128: kc * T + (tt + 1) * 128],
                                                 wb[:, kc * 1024 + nh * 512: kc * 1024 + (nh + 1) * 512],
                                                 start=(kc == 0), stop=(kc == 7))
                    with nc.Block() as blk:
                        @blk.vector
                        def _(e):
                            e.tensor_copy(ev[:, :], ps[:, :])
                    with nc.Block() as blk:
                        @blk.sync
                        def _(e, g0=g0, dst=dst):
                            for j in range(2):
                                tt = g0 + j
                                dma(P_, e, P_.d(dst)[tt * 128:(tt + 1) * 128, :],
                                    ev[:, j * 1024:(j + 1) * 1024])
                            P_.sd.wait(e)


def emit_l1_bias(P_):
    nc = P_.nc
    with ExitStack() as st:
        sb = lambda n, s, d=F32: st.enter_context(nc.sbuf_tensor(P_.uname(n), list(s), d))
        oh = sb("boh", (32, 4224), F32)
        ohb = sb("bohb", (32, 4224), BF16)
        rb = sb("brb", (32, 16), F32)
        rbb = sb("brbb", (32, 16), BF16)
        eb = sb("beb", (16, 4352), F32)
        ps = st.enter_context(nc.psum_tensor(P_.uname("ps_bias"), [16, 512], F32))
        with nc.Block() as blk:
            @blk.sync
            def _(e):
                dma(P_, e, oh[:, :], P_.d("c_onehotT"))
                dma(P_, e, rb[:, :], P_.d("L1_attn_rel_bias"))
                P_.sd.wait(e)
        with nc.Block() as blk:
            @blk.vector
            def _(e):
                e.tensor_copy(ohb[:, :], oh[:, :])
                e.tensor_copy(rbb[:, :], rb[:, :])
                e.memset(eb[:, :], 0.0)
        for c in range(0, 4224, 512):
            w = min(512, 4224 - c)
            with nc.Block() as blk:
                @blk.tensor
                def _(e, c=c, w=w):
                    e.matmul(ps[:, 0:w], rbb[:, :], ohb[:, c:c + w], start=True, stop=True)
            with nc.Block() as blk:
                @blk.vector
                def _(e, c=c, w=w):
                    e.tensor_copy(eb[:, 128 + c: 128 + c + w], ps[:, 0:w])
        with nc.Block() as blk:
            @blk.sync
            def _(e):
                dma(P_, e, P_.d("ebdP"), eb[:, :])
                P_.sd.wait(e)
        with nc.Block() as blk:
            @blk.sync
            def _(e):
                for p in range(128):
                    dma(P_, e, P_.d("STC")[p:p + 1, :, :],
                        P_.d("ebdP")[:, 128 - p: 128 - p + 4224].unsqueeze(0))
                P_.sd.wait(e)


def emit_l1_attn(P_):
    nc = P_.nc
    with ExitStack() as st:
        sb = lambda n, s, d=F32: st.enter_context(nc.sbuf_tensor(P_.uname(n), list(s), d))
        k_sb = sb("a_k", (128, 8 * T), BF16)
        v_sb = sb("a_v", (128, NT * 1024), BF16)
        stair = sb("a_st", (128, 4224), F32)
        q_sb = sb("a_q", (128, S), BF16)
        s_sb = sb("a_s", (128, 2048), F32)
        p_bf = sb("a_p", (128, 2048), BF16)
        pT_sb = sb("a_pT", (128, 16 * 512), BF16)
        cx_sb = sb("a_cx", (64, 512), BF16)
        diag = sb("a_diag", (128, 128), BF16)
        idf = sb("a_idf", (128, 128), F32)
        idb = sb("a_idb", (128, 128), BF16)
        stats = sb("a_stats", (128, 4), F32)
        ps_s = st.enter_context(nc.psum_tensor(P_.uname("ps_as"), [128, 2048], F32))
        ps_p = st.enter_context(nc.psum_tensor(P_.uname("ps_ap"), [128, 2048], F32))
        with nc.Block() as blk:
            @blk.sync
            def _(e):
                dma(P_, e, idf[:, :], P_.d("c_ident"))
                dma(P_, e, k_sb[:, :].rearrange("p (c t) -> p c t", c=8),
                    P_.d("kfm").rearrange("(c p) t -> p c t", p=128))
                dma(P_, e, v_sb[:, :].rearrange("p (t c) -> p t c", t=NT),
                    P_.d("vtm").rearrange("(t p) c -> p t c", p=128))
                P_.sd.wait(e)
        with nc.Block() as blk:
            @blk.vector
            def _(e):
                e.tensor_copy(idb[:, :], idf[:, :])
        for h in range(16):
            with nc.Block() as blk:
                @blk.sync
                def _(e, h=h):
                    dma(P_, e, stair[:, :], P_.d("STC")[:, h, :])
                    P_.sd.wait(e)
            for b in range(BL):
                hs0 = (h % 2) * 64
                with nc.Block() as blk:
                    @blk.sync
                    def _(e, h=h, b=b, hs0=hs0):
                        dma(P_, e, q_sb[hs0:hs0 + 64, :],
                            P_.d("qfm")[h * 64:(h + 1) * 64, b * S:(b + 1) * S])
                        P_.sd.wait(e)
                for qg in range(4):
                    c1b, c2b = P_.c1.count, P_.c2.count
                    sE, sM, sX, sD, sP, sC = P_.e[0], P_.e[1], P_.e[2], P_.e[3], P_.e[4], P_.e[5]
                    sEb, sMb, sXb, sDb, sPb, sCb = (sE.count, sM.count, sX.count,
                                                    sD.count, sP.count, sC.count)
                    with nc.Block() as blk:
                        @blk.tensor
                        def _(e, h=h, b=b, qg=qg):
                            hp, hs = h // 2, (h % 2) * 64
                            for j in range(4):
                                jq = qg * 4 + j
                                if j > 0:
                                    e.wait_ge(sE.h, sEb + j)
                                for kc in range(4):
                                    mm = e.matmul(ps_s[:, kc * 512:(kc + 1) * 512],
                                                  q_sb[hs:hs + 64, jq * 128:(jq + 1) * 128],
                                                  k_sb[hs:hs + 64,
                                                       hp * T + b * S + kc * 512: hp * T + b * S + (kc + 1) * 512],
                                                  start=True, stop=True)
                                mm.then_inc(P_.c1.h, 1)
                                e.wait_ge(sD.h, sDb + j + 1)
                                if j > 0:
                                    e.wait_ge(sP.h, sPb + j)
                                for c in range(16):
                                    mm = e.matmul(ps_p[:, c * 128:(c + 1) * 128],
                                                  p_bf[:, c * 128:(c + 1) * 128],
                                                  diag[:, :], start=True, stop=True)
                                mm.then_inc(P_.c1.h, 1)
                            e.wait_ge(sP.h, sPb + 4)
                            e.wait_ge(sC.h, sCb)
                            for c in range(16):
                                mm = e.matmul(ps_s[0:64, 0:512],
                                              v_sb[:, (b * NTS + c) * 1024 + h * 64: (b * NTS + c) * 1024 + h * 64 + 64],
                                              pT_sb[:, c * 512:(c + 1) * 512],
                                              start=(c == 0), stop=(c == 15))
                            mm.then_inc(P_.c1.h, 1)
                        @blk.vector
                        def _(e, qg=qg):
                            for j in range(4):
                                e.wait_ge(P_.c1.h, c1b + 2 * j + 1)
                                if j > 0:
                                    e.wait_ge(sX.h, sXb + j)
                                i1 = e.tensor_tensor(s_sb[:, :], ps_s[:, :],
                                                     stair[:, 2047 - (qg * 4 + j) * 128: 2047 - (qg * 4 + j) * 128 + 2048],
                                                     ALU.add)
                                i1.then_inc(sE.h, 1)
                                e.wait_ge(sE.h, sEb + j + 1)
                                i2 = e.reduce_max(stats[:, 0:1], s_sb[:, :], AX.X, negate=True)
                                i2.then_inc(sM.h, 1)
                                e.wait_ge(sX.h, sXb + j + 1)
                                i3 = e.reciprocal(stats[:, 2:3], stats[:, 1:2])
                                i3.then_inc(P_.c2.h, 1)
                            e.wait_ge(P_.c1.h, c1b + 9)
                            i6 = e.tensor_copy(cx_sb[:, :], ps_s[0:64, 0:512])
                            i6.then_inc(sC.h, 1)
                        @blk.gpsimd
                        def _(e):
                            for j in range(4):
                                e.wait_ge(P_.c2.h, c2b + j + 1)
                                if j > 0:
                                    e.wait_ge(P_.c1.h, c1b + 2 * j)
                                i4 = e.tensor_tensor(diag[:, :], idb[:, :],
                                                     stats[:, 2:3].broadcast_to([128, 128]), ALU.mult)
                                i4.then_inc(sD.h, 1)
                        @blk.scalar
                        def _(e):
                            for j in range(4):
                                e.wait_ge(sM.h, sMb + j + 1)
                                if j > 0:
                                    e.wait_ge(P_.c1.h, c1b + 2 * j)
                                act = e.activation(p_bf[:, :], s_sb[:, :], AF.Exp,
                                                   bias=stats[:, 0:1], scale=1.0,
                                                   accum_out=stats[:, 1:2])
                                act.then_inc(sX.h, 1)
                                e.wait_ge(P_.c1.h, c1b + 2 * j + 2)
                                act = e.activation(
                                    pT_sb[:, :].rearrange("p (c q) -> p c q", c=16)
                                    [:, :, j * 128:(j + 1) * 128],
                                    ps_p[:, :].rearrange("p (c q) -> p c q", c=16), AF.Identity)
                                act.then_inc(sP.h, 1)
                        @blk.sync
                        def _(e, h=h, b=b, qg=qg):
                            e.wait_ge(sC.h, sCb + 1)
                            dma(P_, e,
                                P_.d("ctxfm")[h * 64:(h + 1) * 64, b * S + qg * 512: b * S + (qg + 1) * 512],
                                cx_sb[:, :])
                            P_.sd.wait(e)
                    sE.count = sEb + 4
                    sM.count = sMb + 4
                    sX.count = sXb + 4
                    sD.count = sDb + 4
                    sP.count = sPb + 4
                    sC.count = sCb + 1
                    P_.c1.count = c1b + 9
                    P_.c2.count = c2b + 4
    with ExitStack() as st:
        sb = lambda n, s, d=F32: st.enter_context(nc.sbuf_tensor(P_.uname(n), list(s), d))
        ctx = sb("wo_ctx", (128, 8 * T), BF16)
        wob = sb("wo_w", (128, 8 * 1024), BF16)
        ev = sb("wo_ev", (128, 2 * 1024), F32)
        ps = st.enter_context(nc.psum_tensor(P_.uname("ps_wo"), [128, 2 * 1024], F32))
        with nc.Block() as blk:
            @blk.sync
            def _(e):
                dma(P_, e, ctx[:, :].rearrange("p (c t) -> p c t", c=8),
                    P_.d("ctxfm").rearrange("(c p) t -> p c t", p=128))
                P_.sd.wait(e)
            @blk.gpsimd
            def _(e):
                for part in range(4):
                    dma(P_, e,
                        wob[:, :].rearrange("p (k c) -> p k c", k=8)
                        [:, part * 2:(part + 1) * 2, :],
                        P_.d("L1_attn_Wo").rearrange("(k p) c -> p k c", p=128)
                        [:, part * 2:(part + 1) * 2, :], sem=P_.sg)
                P_.sg.wait(e)
        for g0 in range(0, NT, 2):
            with nc.Block() as blk:
                @blk.tensor
                def _(e, g0=g0):
                    for j in range(2):
                        tt = g0 + j
                        for nh in range(2):
                            for kc in range(8):
                                e.matmul(ps[:, j * 1024 + nh * 512: j * 1024 + (nh + 1) * 512],
                                         ctx[:, kc * T + tt * 128: kc * T + (tt + 1) * 128],
                                         wob[:, kc * 1024 + nh * 512: kc * 1024 + (nh + 1) * 512],
                                         start=(kc == 0), stop=(kc == 7))
            with nc.Block() as blk:
                @blk.vector
                def _(e):
                    e.tensor_copy(ev[:, :], ps[:, :])
            with nc.Block() as blk:
                @blk.sync
                def _(e, g0=g0):
                    for j in range(2):
                        tt = g0 + j
                        dma(P_, e, P_.d("aout")[tt * 128:(tt + 1) * 128, :],
                            ev[:, j * 1024:(j + 1) * 1024])
                    P_.sd.wait(e)


def emit_l1_ffn(P_, nlayers=6, parts='ab'):
    nc = P_.nc
    with ExitStack() as st:
        sb = lambda n, s, d=F32: st.enter_context(nc.sbuf_tensor(P_.uname(n), list(s), d))
        csq = sb("f_csq", (128, 1024)); snq = sb("f_snq", (128, 1024))
        csk = sb("f_csk", (128, 1024)); snk = sb("f_snk", (128, 1024))
        masks = sb("f_masks", (128, 3 * 384))
        idb = sb("f_idb", (128, 128), BF16)
        ones_bf = sb("f_ones", (1, 128), BF16)
        stgid = sb("f_stgid", (128, 128))
        with nc.Block() as blk:
            @blk.sync
            def _(e):
                dma(P_, e, csq[:, :], P_.d("c_csq"))
                dma(P_, e, snq[:, :], P_.d("c_snq"))
                dma(P_, e, csk[:, :], P_.d("c_csk"))
                dma(P_, e, snk[:, :], P_.d("c_snk"))
                for m in range(3):
                    dma(P_, e, masks[:, m * 384:(m + 1) * 384], P_.d(f"c_mask{m}"))
                dma(P_, e, stgid[:, :], P_.d("c_ident"))
                P_.sd.wait(e)
        with nc.Block() as blk:
            @blk.vector
            def _(e):
                e.tensor_copy(idb[:, :], stgid[:, :])
                e.memset(ones_bf[:, :], 1.0)
        for li in range(nlayers):
            _l1ffn_layer(P_, li, csq, snq, csk, snk, masks, idb, ones_bf, parts)


def _l1ffn_layer(P_, li, csq, snq, csk, snk, masks, idb, ones_bf, parts='ab'):
    nc = P_.nc
    with ExitStack() as st:
        sb = lambda n, s, d=F32: st.enter_context(nc.sbuf_tensor(P_.uname(n), list(s), d))
        csqs = sb("f_csqs", (128, 1024)); snqs = sb("f_snqs", (128, 1024))
        cski = sb("f_cski", (128, 1024)); snki = sb("f_snki", (128, 1024))
        grep = sb("f_grep", (128, 1024)); brep = sb("f_brep", (128, 1024))
        b2r = sb("f_b2r", (1, 1024), BF16)
        b1p = sb("f_b1p", (128, 44))
        qsr = sb("f_qsr", (128, 2 * 64))
        ksr = sb("f_ksr", (128, 2 * 64))
        stg1 = b1p
        stg1 = sb("f_stg1", (1, 1024))
        with nc.Block() as blk:
            @blk.sync
            def _(e, li=li):
                dma(P_, e, qsr[:, 0:64], P_.d("L1_ffn_q_scale")[li:li + 1, :].broadcast_to([128, 64]))
                dma(P_, e, ksr[:, 0:64], P_.d("L1_ffn_k_scale")[li:li + 1, :].broadcast_to([128, 64]))
                dma(P_, e, grep[:, :], P_.d("L1_ffn_ln_g")[li:li + 1, :].broadcast_to([128, 1024]))
                dma(P_, e, brep[:, :], P_.d("L1_ffn_ln_b")[li:li + 1, :].broadcast_to([128, 1024]))
                dma(P_, e, b1p[:, :], P_.d("B1P")[li])
                dma(P_, e, stg1[:, :], P_.d("L1_ffn_b2")[li:li + 1, :])
                P_.sd.wait(e)
        with nc.Block() as blk:
            @blk.vector
            def _(e):
                e.tensor_copy(qsr[:, 64:96], qsr[:, 32:64])
                e.tensor_copy(qsr[:, 96:128], qsr[:, 0:32])
                e.tensor_copy(ksr[:, 64:96], ksr[:, 32:64])
                e.tensor_copy(ksr[:, 96:128], ksr[:, 0:32])
                e.tensor_copy(b2r[:, :], stg1[:, :])
        with nc.Block() as blk:
            @blk.vector
            def _(e):
                q3 = lambda t: t.rearrange("p (j d) -> p j d", d=64)
                bc = lambda t: t.unsqueeze(1).broadcast_to([128, 16, 64])
                e.tensor_tensor(q3(csqs), q3(csq), bc(qsr[:, 0:64]), ALU.mult)
                e.tensor_tensor(q3(snqs), q3(snq), bc(qsr[:, 64:128]), ALU.mult)
                e.tensor_tensor(q3(cski), q3(csk), bc(ksr[:, 0:64]), ALU.mult)
                i = e.tensor_tensor(q3(snki), q3(snk), bc(ksr[:, 64:128]), ALU.mult)
                FV(P_, e, i)
                h32t = lambda t: t.rearrange("p (j d) -> p j d", d=64)[:, :, 0:32]
                e.tensor_scalar_mul(h32t(snqs), h32t(snqs), -1.0)
                e.tensor_scalar_mul(h32t(snki), h32t(snki), -1.0)

        with ExitStack() as st2:
            sb2 = lambda n, s, d=F32: st2.enter_context(nc.sbuf_tensor(P_.uname(n), list(s), d))
            xtt = sb2("f_xtt", (128, 2 * 1024), BF16)
            wqkv = sb2("f_wqkv", (128, 8 * 3072), BF16)
            xbuf = sb2("f_xbuf", (128, 1024))
            xsb = sb2("f_xsb", (128, 1024), BF16)
            ps = st2.enter_context(nc.psum_tensor(P_.uname("ps_fq"), [128, 3072], F32))
            qf = sb2("f_qf", (128, 2 * 2048))
            sq2 = sb2("f_sq2", (128, 2 * 2048))
            ss = sb2("f_ss", (128, 2 * 64))
            qn = sb2("f_qn", (128, 2 * 2048))
            qkout = sb2("f_qkout", (128, 2 * 2048), BF16)
            vsb = sb2("f_vsb", (128, 2 * 1024), BF16)
            with nc.Block() as blk:
                @blk.gpsimd
                def _(e, li=li):
                    for part in range(4):
                        dma(P_, e,
                            wqkv[:, :].rearrange("p (k c) -> p k c", k=8)
                            [:, part * 2:(part + 1) * 2, :],
                            P_.d("L1_ffn_Wqkv")[li].rearrange("(k p) c -> p k c", p=128)
                            [:, part * 2:(part + 1) * 2, :], sem=P_.sg)
                    P_.sg.wait(e)
            for tt in range(NT):
                first = (tt % NTS == 0)
                with nc.Block() as blk:
                    @blk.sync
                    def _(e, tt=tt, first=first):
                        r0 = tt * 128
                        dma(P_, e, xbuf[:, 0:512], P_.d("xi")[r0:r0 + 128, 0:512])
                        if first:
                            dma(P_, e, xbuf[1:128, 512:1024], P_.d("xi")[r0:r0 + 127, 512:1024])
                        else:
                            dma(P_, e, xbuf[:, 512:1024], P_.d("xi")[r0 - 1:r0 + 127, 512:1024])
                        P_.sd.wait(e)
                    @blk.vector
                    def _(e, first=first):
                        if first:
                            e.memset(xbuf[0:1, 512:1024], 0.0)
                with nc.Block() as blk:
                    @blk.vector
                    def _(e):
                        e.tensor_copy(xsb[:, :], xbuf[:, :])
                with nc.Block() as blk:
                    @blk.sync
                    def _(e, tt=tt):
                        dma(P_, e, P_.d("xsbf")[tt * 128:(tt + 1) * 128, :], xsb[:, :])
                        P_.sd.wait(e)
            NTT = NT
            c1b, c2b, c3b, c4b = P_.c1.count, P_.c2.count, P_.c3.count, P_.c4.count
            e0b, e1b, e6b, e7b = P_.e[0].count, P_.e[1].count, P_.e[6].count, P_.e[7].count
            sdb, ssb = P_.sd.count, P_.ss.count
            with nc.Block() as blk:
                @blk.sync
                def _(e):
                    for u in range(NTT):
                        sl = u % 2
                        if u >= 2:
                            e.wait_ge(P_.c1.h, c1b + (u - 1))
                        for kc in range(8):
                            e.dma_start(xtt[:, sl * 1024 + kc * 128: sl * 1024 + (kc + 1) * 128],
                                        P_.d("xsbf")[u * 128:(u + 1) * 128, kc * 128:(kc + 1) * 128],
                                        transpose=True).then_inc(P_.sd.h, 16)
                        if u >= 1:
                            up, slp = u - 1, (u - 1) % 2
                            e.wait_ge(P_.e[7].h, e7b + 2 * up + 2)
                            e.dma_start(P_.d("qtm")[up * 128:(up + 1) * 128, :],
                                        qkout[:, slp * 2048: slp * 2048 + 1024]).then_inc(P_.ss.h, 16)
                            e.dma_start(P_.d("ktm")[up * 128:(up + 1) * 128, :],
                                        qkout[:, slp * 2048 + 1024: slp * 2048 + 2048]).then_inc(P_.ss.h, 16)
                            e.dma_start(P_.d("vtm")[up * 128:(up + 1) * 128, :],
                                        vsb[:, slp * 1024:(slp + 1) * 1024]).then_inc(P_.ss.h, 16)
                    up, slp = NTT - 1, (NTT - 1) % 2
                    e.wait_ge(P_.e[7].h, e7b + 2 * up + 2)
                    e.dma_start(P_.d("qtm")[up * 128:(up + 1) * 128, :],
                                qkout[:, slp * 2048: slp * 2048 + 1024]).then_inc(P_.ss.h, 16)
                    e.dma_start(P_.d("ktm")[up * 128:(up + 1) * 128, :],
                                qkout[:, slp * 2048 + 1024: slp * 2048 + 2048]).then_inc(P_.ss.h, 16)
                    e.dma_start(P_.d("vtm")[up * 128:(up + 1) * 128, :],
                                vsb[:, slp * 1024:(slp + 1) * 1024]).then_inc(P_.ss.h, 16)
                    e.wait_ge(P_.sd.h, sdb + 16 * 8 * NTT)
                    e.wait_ge(P_.ss.h, ssb + 16 * 3 * NTT)
                @blk.tensor
                def _(e):
                    for u in range(NTT):
                        sl = u % 2
                        e.wait_ge(P_.sd.h, sdb + 16 * 8 * (u + 1))
                        if u >= 1:
                            e.wait_ge(P_.c2.h, c2b + u)
                        for nh in range(6):
                            for kc in range(8):
                                mm = e.matmul(ps[:, nh * 512:(nh + 1) * 512],
                                              xtt[:, sl * 1024 + kc * 128: sl * 1024 + (kc + 1) * 128],
                                              wqkv[:, kc * 3072 + nh * 512: kc * 3072 + (nh + 1) * 512],
                                              start=(kc == 0), stop=(kc == 7))
                        mm.then_inc(P_.c1.h, 1)
                @blk.scalar
                def _(e):
                    for u in range(NTT):
                        sl = u % 2
                        e.wait_ge(P_.c1.h, c1b + u + 1)
                        if u >= 2:
                            e.wait_ge(P_.ss.h, ssb + 16 * 3 * (u - 1))
                            e.wait_ge(P_.e[7].h, e7b + 2 * (u - 2) + 2)
                        i = e.activation(qf[:, sl * 2048: sl * 2048 + 1024], ps[:, 0:1024], AF.Identity)
                        FS(P_, e, i)
                        i = e.activation(qf[:, sl * 2048 + 1024: sl * 2048 + 2048],
                                         ps[:, 1024:2048], AF.Identity)
                        FS(P_, e, i)
                        i = e.activation(vsb[:, sl * 1024:(sl + 1) * 1024], ps[:, 2048:3072], AF.Identity)
                        i.then_inc(P_.c2.h, 1)
                        e.activation(sq2[:, sl * 2048: sl * 2048 + 1024],
                                     qf[:, sl * 2048: sl * 2048 + 1024], AF.Square)
                        i = e.activation(sq2[:, sl * 2048 + 1024: sl * 2048 + 2048],
                                         qf[:, sl * 2048 + 1024: sl * 2048 + 2048], AF.Square)
                        i.then_inc(P_.c3.h, 1)
                        e.wait_ge(P_.c4.h, c4b + u + 1)
                        i = e.activation(ss[:, sl * 64 + 32: sl * 64 + 64], ss[:, sl * 64: sl * 64 + 32], AF.Ln)
                        FS(P_, e, i)
                        i = e.activation(ss[:, sl * 64 + 32: sl * 64 + 64],
                                         ss[:, sl * 64 + 32: sl * 64 + 64], AF.Exp, scale=-0.5)
                        i.then_inc(P_.e[6].h, 1)
                @blk.vector
                def _(e):
                    h3 = lambda t, off: t[:, off:off + 1024].rearrange("p (j d) -> p j d", d=64)
                    bch = lambda t: t.unsqueeze(2).broadcast_to([128, 16, 64])
                    bcd = lambda t: t.unsqueeze(1).broadcast_to([128, 16, 64])
                    for u in range(NTT):
                        sl = u % 2
                        jt = u % NTS
                        e.wait_ge(P_.c3.h, c3b + u + 1)
                        if u >= 2:
                            e.wait_ge(P_.e[6].h, e6b + (u - 1))
                            e.wait_ge(P_.e[1].h, e1b + 2 * (u - 2) + 2)
                        e.tensor_reduce(ss[:, sl * 64: sl * 64 + 16], h3(sq2, sl * 2048), AX.X, ALU.add)
                        i = e.tensor_reduce(ss[:, sl * 64 + 16: sl * 64 + 32],
                                            h3(sq2, sl * 2048 + 1024), AX.X, ALU.add)
                        FV(P_, e, i)
                        i = e.tensor_scalar_max(ss[:, sl * 64: sl * 64 + 32],
                                                ss[:, sl * 64: sl * 64 + 32], 1e-24)
                        i.then_inc(P_.c4.h, 1)
                        e.wait_ge(P_.e[6].h, e6b + u + 1)
                        i = e.tensor_tensor(h3(qn, sl * 2048), h3(qf, sl * 2048),
                                            bch(ss[:, sl * 64 + 32: sl * 64 + 48]), ALU.mult)
                        i.then_inc(P_.e[0].h, 1)
                        e.wait_ge(P_.e[0].h, e0b + 2 * u + 1)
                        i = e.tensor_tensor(h3(qn, sl * 2048 + 1024), h3(qf, sl * 2048 + 1024),
                                            bch(ss[:, sl * 64 + 48: sl * 64 + 64]), ALU.mult)
                        i.then_inc(P_.e[0].h, 1)
                        e.wait_ge(P_.e[0].h, e0b + 2 * u + 2)
                        e.tensor_tensor(h3(qf, sl * 2048), h3(qn, sl * 2048),
                                        bcd(csqs[:, jt * 64: jt * 64 + 64]), ALU.mult)
                        i = e.tensor_tensor(h3(qf, sl * 2048 + 1024), h3(qn, sl * 2048 + 1024),
                                            bcd(cski[:, jt * 64: jt * 64 + 64]), ALU.mult)
                        FV(P_, e, i)
                        if u >= 2:
                            e.wait_ge(P_.ss.h, ssb + 16 * 3 * (u - 1))
                        e.wait_ge(P_.e[1].h, e1b + 2 * u + 1)
                        i = e.tensor_tensor(qkout[:, sl * 2048: sl * 2048 + 1024],
                                            qf[:, sl * 2048: sl * 2048 + 1024],
                                            sq2[:, sl * 2048: sl * 2048 + 1024], ALU.add)
                        i.then_inc(P_.e[7].h, 1)
                        e.wait_ge(P_.e[1].h, e1b + 2 * u + 2)
                        i = e.tensor_tensor(qkout[:, sl * 2048 + 1024: sl * 2048 + 2048],
                                            qf[:, sl * 2048 + 1024: sl * 2048 + 2048],
                                            sq2[:, sl * 2048 + 1024: sl * 2048 + 2048], ALU.add)
                        i.then_inc(P_.e[7].h, 1)
                @blk.gpsimd
                def _(e):
                    h32o = lambda off, lo: sq2[:, off:off + 1024].rearrange(
                        "p (j d) -> p j d", d=64)[:, :, lo:lo + 32]
                    h32i = lambda off, lo: qn[:, off:off + 1024].rearrange(
                        "p (j d) -> p j d", d=64)[:, :, lo:lo + 32]
                    bcd32 = lambda t: t.unsqueeze(1).broadcast_to([128, 16, 32])
                    for u in range(NTT):
                        sl = u % 2
                        jt = u % NTS
                        e.wait_ge(P_.e[0].h, e0b + 2 * u + 1)
                        e.tensor_tensor(h32o(sl * 2048, 0), h32i(sl * 2048, 32),
                                        bcd32(snqs[:, jt * 64: jt * 64 + 32]), ALU.mult)
                        i = e.tensor_tensor(h32o(sl * 2048, 32), h32i(sl * 2048, 0),
                                            bcd32(snqs[:, jt * 64 + 32: jt * 64 + 64]), ALU.mult)
                        i.then_inc(P_.e[1].h, 1)
                        e.wait_ge(P_.e[0].h, e0b + 2 * u + 2)
                        e.tensor_tensor(h32o(sl * 2048 + 1024, 0), h32i(sl * 2048 + 1024, 32),
                                        bcd32(snki[:, jt * 64: jt * 64 + 32]), ALU.mult)
                        i = e.tensor_tensor(h32o(sl * 2048 + 1024, 32), h32i(sl * 2048 + 1024, 0),
                                            bcd32(snki[:, jt * 64 + 32: jt * 64 + 64]), ALU.mult)
                        i.then_inc(P_.e[1].h, 1)
            P_.c1.count = c1b + NTT
            P_.c2.count = c2b + NTT
            P_.c3.count = c3b + NTT
            P_.c4.count = c4b + NTT
            P_.e[0].count = e0b + 2 * NTT
            P_.e[1].count = e1b + 2 * NTT
            P_.e[6].count = e6b + NTT
            P_.e[7].count = e7b + 2 * NTT
            P_.sd.count = sdb + 16 * 8 * NTT
            P_.ss.count = ssb + 16 * 3 * NTT

        with ExitStack() as st2:
            sb2 = lambda n, s, d=F32: st2.enter_context(nc.sbuf_tensor(P_.uname(n), list(s), d))
            v_sb = sb2("la_v", (128, NTS * 1024), BF16)
            qT = sb2("la_qT", (128, 2 * S), BF16)
            kT = sb2("la_kT", (128, 2 * (256 + S)), BF16)
            s_sb = sb2("la_s", (128, 2 * 384))
            p_bf = sb2("la_p", (128, 2 * 384), BF16)
            pT_sb = sb2("la_pT", (128, 2 * 384), BF16)
            diag = sb2("la_diag", (128, 2 * 128), BF16)
            cxb = sb2("la_cx", (64, 2 * 128), BF16)
            stats = sb2("la_stats", (128, 2 * 4))
            ps_s = st2.enter_context(nc.psum_tensor(P_.uname("ps_ls"), [128, 2 * 512], F32))
            ps_p = st2.enter_context(nc.psum_tensor(P_.uname("ps_lp"), [128, 2 * 512], F32))
            ps_c = st2.enter_context(nc.psum_tensor(P_.uname("ps_lc"), [64, 2 * 128], F32))
            qk_marks = {}
            for b in range(BL):
                with nc.Block() as blk:
                    @blk.sync
                    def _(e, b=b):
                        dma(P_, e, v_sb[:, :].rearrange("p (t c) -> p t c", t=NTS),
                            P_.d("vtm")[b * S:(b + 1) * S, :].rearrange("(t p) c -> p t c", p=128))
                        P_.sd.wait(e)
                for hp in range(8):
                    bsl = hp % 2
                    if hp == 0:
                        with nc.Block() as blk:
                            @blk.sync
                            def _(e, b=b):
                                for tt in range(NTS):
                                    dma(P_, e, qT[:, tt * 128:(tt + 1) * 128],
                                        P_.d("qtm")[b * S + tt * 128: b * S + (tt + 1) * 128,
                                                    0:128], transpose=True)
                                    dma(P_, e, kT[:, 256 + tt * 128: 256 + (tt + 1) * 128],
                                        P_.d("ktm")[b * S + tt * 128: b * S + (tt + 1) * 128,
                                                    0:128], transpose=True)
                                P_.sd.wait(e)
                                qk_marks[(b, 0)] = P_.sd.count
                            @blk.vector
                            def _(e):
                                e.memset(kT[:, 0:256], 0.0)
                                e.memset(kT[:, 2304:2304 + 256], 0.0)
                    c1b, c2b = P_.c1.count, P_.c2.count
                    eb = [P_.e[i].count for i in range(6)]
                    ssb_a4 = P_.ss.count
                    with nc.Block() as blk:
                        @blk.tensor
                        def _(e, b=b, hp=hp, bsl=bsl):
                            e.wait_ge(P_.sd.h, qk_marks[(b, hp)])
                            for u in range(32):
                                j, h2 = u // 2, u % 2
                                sl = u % 2
                                hs = h2 * 64
                                if u >= 2:
                                    e.wait_ge(P_.e[0].h, eb[0] + (u - 1))
                                mm = e.matmul(ps_s[:, sl * 512: sl * 512 + 384],
                                              qT[hs:hs + 64, bsl * S + j * 128: bsl * S + (j + 1) * 128],
                                              kT[hs:hs + 64, bsl * 2304 + 128 * j: bsl * 2304 + 128 * j + 384],
                                              start=True, stop=True)
                                mm.then_inc(P_.c1.h, 1)
                                e.wait_ge(P_.e[3].h, eb[3] + u + 1)
                                if u >= 2:
                                    e.wait_ge(P_.e[4].h, eb[4] + (u - 1))
                                for c in range(3):
                                    mm = e.matmul(ps_p[:, sl * 512 + c * 128: sl * 512 + (c + 1) * 128],
                                                  p_bf[:, sl * 384 + c * 128: sl * 384 + (c + 1) * 128],
                                                  diag[:, sl * 128:(sl + 1) * 128],
                                                  start=True, stop=True)
                                mm.then_inc(P_.c1.h, 1)
                                e.wait_ge(P_.e[4].h, eb[4] + u + 1)
                                if u >= 2:
                                    e.wait_ge(P_.e[5].h, eb[5] + (u - 1))
                                h = hp * 2 + h2
                                valid = [c for c in range(3) if j - 2 + c >= 0]
                                for ci, c in enumerate(valid):
                                    kt = j - 2 + c
                                    mm = e.matmul(ps_c[:, sl * 128:(sl + 1) * 128],
                                                  v_sb[:, kt * 1024 + h * 64: kt * 1024 + h * 64 + 64],
                                                  pT_sb[:, sl * 384 + c * 128: sl * 384 + (c + 1) * 128],
                                                  start=(ci == 0), stop=(ci == len(valid) - 1))
                                mm.then_inc(P_.c1.h, 1)
                        @blk.vector
                        def _(e):
                            for u in range(32):
                                j = u // 2
                                sl = u % 2
                                mi = min(j, 2)
                                e.wait_ge(P_.c1.h, c1b + 3 * u + 1)
                                if u >= 2:
                                    e.wait_ge(P_.e[2].h, eb[2] + (u - 1))
                                i = e.scalar_tensor_tensor(s_sb[:, sl * 384:(sl + 1) * 384],
                                                           ps_s[:, sl * 512: sl * 512 + 384], 8.0,
                                                           masks[:, mi * 384:(mi + 1) * 384],
                                                           ALU.mult, ALU.add)
                                i.then_inc(P_.e[0].h, 1)
                                e.wait_ge(P_.e[0].h, eb[0] + u + 1)
                                i = e.reduce_max(stats[:, sl * 4: sl * 4 + 1],
                                                 s_sb[:, sl * 384:(sl + 1) * 384], AX.X, negate=True)
                                i.then_inc(P_.e[1].h, 1)
                                e.wait_ge(P_.e[2].h, eb[2] + u + 1)
                                i = e.reciprocal(stats[:, sl * 4 + 2: sl * 4 + 3],
                                                 stats[:, sl * 4 + 1: sl * 4 + 2])
                                i.then_inc(P_.c2.h, 1)
                                e.wait_ge(P_.c1.h, c1b + 3 * u + 3)
                                if u >= 2:
                                    e.wait_ge(P_.ss.h, ssb_a4 + 16 * (u - 1))
                                i = e.tensor_copy(cxb[:, sl * 128:(sl + 1) * 128],
                                                  ps_c[:, sl * 128:(sl + 1) * 128])
                                i.then_inc(P_.e[5].h, 1)
                        @blk.gpsimd
                        def _(e):
                            for u in range(32):
                                sl = u % 2
                                e.wait_ge(P_.c2.h, c2b + u + 1)
                                if u >= 2:
                                    e.wait_ge(P_.c1.h, c1b + 3 * (u - 2) + 2)
                                i = e.tensor_tensor(diag[:, sl * 128:(sl + 1) * 128], idb[:, :],
                                                    stats[:, sl * 4 + 2: sl * 4 + 3]
                                                    .broadcast_to([128, 128]), ALU.mult)
                                i.then_inc(P_.e[3].h, 1)
                        @blk.scalar
                        def _(e):
                            for u in range(32):
                                sl = u % 2
                                e.wait_ge(P_.e[1].h, eb[1] + u + 1)
                                if u >= 2:
                                    e.wait_ge(P_.c1.h, c1b + 3 * (u - 2) + 2)
                                act = e.activation(p_bf[:, sl * 384:(sl + 1) * 384],
                                                   s_sb[:, sl * 384:(sl + 1) * 384], AF.Exp,
                                                   bias=stats[:, sl * 4: sl * 4 + 1], scale=1.0,
                                                   accum_out=stats[:, sl * 4 + 1: sl * 4 + 2])
                                act.then_inc(P_.e[2].h, 1)
                                e.wait_ge(P_.c1.h, c1b + 3 * u + 2)
                                act = e.activation(pT_sb[:, sl * 384:(sl + 1) * 384],
                                                   ps_p[:, sl * 512: sl * 512 + 384], AF.Identity)
                                act.then_inc(P_.e[4].h, 1)
                        @blk.sync
                        def _(e, b=b, hp=hp, bsl=bsl):
                            if hp < 7:
                                nsl = 1 - bsl
                                for tt in range(NTS):
                                    dma(P_, e, qT[:, nsl * S + tt * 128: nsl * S + (tt + 1) * 128],
                                        P_.d("qtm")[b * S + tt * 128: b * S + (tt + 1) * 128,
                                                    (hp + 1) * 128:(hp + 2) * 128], transpose=True)
                                    dma(P_, e, kT[:, nsl * 2304 + 256 + tt * 128: nsl * 2304 + 256 + (tt + 1) * 128],
                                        P_.d("ktm")[b * S + tt * 128: b * S + (tt + 1) * 128,
                                                    (hp + 1) * 128:(hp + 2) * 128], transpose=True)
                                qk_marks[(b, hp + 1)] = P_.sd.count
                            for u in range(32):
                                j, h2 = u // 2, u % 2
                                sl = u % 2
                                h = hp * 2 + h2
                                e.wait_ge(P_.e[5].h, eb[5] + u + 1)
                                dma(P_, e, P_.d("ctxfm")[h * 64:(h + 1) * 64,
                                                         b * S + j * 128: b * S + (j + 1) * 128],
                                    cxb[:, sl * 128:(sl + 1) * 128], sem=P_.ss)
                            P_.sd.wait(e)
                            P_.ss.wait(e)
                    P_.c1.count = c1b + 96
                    P_.c2.count = c2b + 32
                    for i in range(6):
                        P_.e[i].count = eb[i] + 32

        with ExitStack() as st2:
            sb2 = lambda n, s, d=F32: st2.enter_context(nc.sbuf_tensor(P_.uname(n), list(s), d))
            ctx = sb2("wo2_ctx", (128, 8 * T), BF16)
            wob = sb2("wo2_w", (128, 8 * 1024), BF16)
            ev = sb2("wo2_ev", (128, 2 * 1024))
            xit = sb2("wo2_xi", (128, 2 * 1024))
            ps = st2.enter_context(nc.psum_tensor(P_.uname("ps_wo2"), [128, 2 * 1024], F32))
            with nc.Block() as blk:
                @blk.sync
                def _(e, li=li):
                    dma(P_, e, ctx[:, :].rearrange("p (c t) -> p c t", c=8),
                        P_.d("ctxfm").rearrange("(c p) t -> p c t", p=128))
                    P_.sd.wait(e)
                @blk.gpsimd
                def _(e, li=li):
                    for part in range(4):
                        dma(P_, e,
                            wob[:, :].rearrange("p (k c) -> p k c", k=8)
                            [:, part * 2:(part + 1) * 2, :],
                            P_.d("L1_ffn_Wo")[li].rearrange("(k p) c -> p k c", p=128)
                            [:, part * 2:(part + 1) * 2, :], sem=P_.sg)
                    P_.sg.wait(e)
            for g0 in range(0, NT, 2):
                with nc.Block() as blk:
                    @blk.sync
                    def _(e, g0=g0):
                        for j in range(2):
                            tt = g0 + j
                            dma(P_, e, xit[:, j * 1024:(j + 1) * 1024],
                                P_.d("xi")[tt * 128:(tt + 1) * 128, :])
                        P_.sd.wait(e)
                    @blk.tensor
                    def _(e, g0=g0):
                        for j in range(2):
                            tt = g0 + j
                            for nh in range(2):
                                for kc in range(8):
                                    e.matmul(ps[:, j * 1024 + nh * 512: j * 1024 + (nh + 1) * 512],
                                             ctx[:, kc * T + tt * 128: kc * T + (tt + 1) * 128],
                                             wob[:, kc * 1024 + nh * 512: kc * 1024 + (nh + 1) * 512],
                                             start=(kc == 0), stop=(kc == 7))
                with nc.Block() as blk:
                    @blk.vector
                    def _(e):
                        e.tensor_tensor(ev[:, :], ps[:, :], xit[:, :], ALU.add)
                with nc.Block() as blk:
                    @blk.sync
                    def _(e, g0=g0):
                        for j in range(2):
                            tt = g0 + j
                            dma(P_, e, P_.d("xi")[tt * 128:(tt + 1) * 128, :],
                                ev[:, j * 1024:(j + 1) * 1024])
                        P_.sd.wait(e)

        if 'b' not in parts:
            return
        with ExitStack() as st2:
            sb2 = lambda n, s, d=F32: st2.enter_context(nc.sbuf_tensor(P_.uname(n), list(s), d))
            xbuf = sb2("b1_x", (128, 1024))
            scr = sb2("b1_scr", (128, 1024))
            tn = sb2("b1_t", (128, 1024))
            ob = sb2("b1_o", (128, 1024), BF16)
            sl_ = sb2("b1_sl", (128, 8))
            for tt in range(NT):
                first = (tt % NTS == 0)
                c3b, c4b, e6b, e7b = P_.c3.count, P_.c4.count, P_.e[6].count, P_.e[7].count
                with nc.Block() as blk:
                    @blk.sync
                    def _(e, tt=tt, first=first):
                        r0 = tt * 128
                        dma(P_, e, xbuf[:, 0:512], P_.d("xi")[r0:r0 + 128, 0:512])
                        if first:
                            dma(P_, e, xbuf[1:128, 512:1024], P_.d("xi")[r0:r0 + 127, 512:1024])
                        else:
                            dma(P_, e, xbuf[:, 512:1024], P_.d("xi")[r0 - 1:r0 + 127, 512:1024])
                        P_.sd.wait(e)
                    @blk.vector
                    def _(e, first=first):
                        if first:
                            e.memset(xbuf[0:1, 512:1024], 0.0)
                with nc.Block() as blk:
                    @blk.vector
                    def _(e):
                        i = e.reduce_sum(sl_[:, 0:1], xbuf[:, :], AX.X)
                        FV(P_, e, i)
                        i = e.tensor_scalar_mul(sl_[:, 1:2], sl_[:, 0:1], -1.0 / D)
                        i.then_inc(P_.c4.h, 1)
                        e.wait_ge(P_.c3.h, c3b + 1)
                        i = e.tensor_scalar(sl_[:, 3:4], sl_[:, 2:3], 1.0 / D, 1e-5, ALU.mult, ALU.add)
                        i.then_inc(P_.c4.h, 1)
                        e.wait_ge(P_.c3.h, c3b + 2)
                        i = e.reciprocal(sl_[:, 5:6], sl_[:, 4:5])
                        FV(P_, e, i)
                        i = e.tensor_mul(sl_[:, 6:7], sl_[:, 1:2], sl_[:, 5:6])
                        i.then_inc(P_.c4.h, 1)
                        e.wait_ge(P_.c3.h, c3b + 3)
                        i = e.tensor_tensor(tn[:, :], tn[:, :], grep[:, :], ALU.mult)
                        FV(P_, e, i)
                        i = e.tensor_tensor(ob[:, :], tn[:, :], brep[:, :], ALU.add)
                        i.then_inc(P_.e[6].h, 1)
                    @blk.scalar
                    def _(e):
                        e.wait_ge(P_.c4.h, c4b + 1)
                        i = e.activation(scr[:, :], xbuf[:, :], AF.Square, bias=sl_[:, 1:2],
                                         scale=1.0, accum_out=sl_[:, 2:3])
                        i.then_inc(P_.c3.h, 1)
                        e.wait_ge(P_.c4.h, c4b + 2)
                        i = e.activation(sl_[:, 4:5], sl_[:, 3:4], AF.Sqrt)
                        i.then_inc(P_.c3.h, 1)
                        e.wait_ge(P_.c4.h, c4b + 3)
                        i = e.activation(tn[:, :], xbuf[:, :], AF.Identity, bias=sl_[:, 6:7],
                                         scale=sl_[:, 5:6])
                        i.then_inc(P_.c3.h, 1)
                    @blk.sync
                    def _(e, tt=tt):
                        e.wait_ge(P_.e[6].h, e6b + 1)
                        dma(P_, e, P_.d("xsbf")[tt * 128:(tt + 1) * 128, :], ob[:, :])
                        P_.sd.wait(e)
                P_.c3.count = c3b + 3
                P_.c4.count = c4b + 3
                P_.e[6].count = e6b + 1

        with ExitStack() as st2:
            sb2 = lambda n, s, d=F32: st2.enter_context(nc.sbuf_tensor(P_.uname(n), list(s), d))
            w2pb = sb2("b3_w2", (128, 22 * 1024), BF16)
            hbuf = sb2("b3_h", (128, 22 * 1024), BF16)
            xt = sb2("b3_xt", (128, 8 * 1024), BF16)
            w1b = sb2("b3_w1b", (128, 2 * 8 * 512), BF16)
            gel = sb2("b3_gel", (128, 2 * 512))
            ev = sb2("b3_ev", (128, 2 * 1024))
            xit = sb2("b3_xi", (128, 2 * 1024))
            ps_a = st2.enter_context(nc.psum_tensor(P_.uname("ps_ba"), [128, 2 * 512], F32))
            ps_g = st2.enter_context(nc.psum_tensor(P_.uname("ps_bg"), [128, 2 * 512], F32))
            ps_y = st2.enter_context(nc.psum_tensor(P_.uname("ps_by"), [128, 2 * 1024], F32))
            with nc.Block() as blk:
                @blk.gpsimd
                def _(e, li=li):
                    for part in range(4):
                        lo, hi = part * 6, min(22, (part + 1) * 6)
                        dma(P_, e,
                            w2pb[:, :].rearrange("p (k c) -> p k c", k=22)[:, lo:hi, :],
                            P_.d("W2P")[li].rearrange("(k p) c -> p k c", p=128)[:, lo:hi, :],
                            sem=P_.sg)
                    P_.sg.wait(e)
            for tc in range(4):
                with nc.Block() as blk:
                    @blk.sync
                    def _(e, tc=tc):
                        for ct in range(8):
                            for rt in range(8):
                                dma(P_, e, xt[:, ct * 1024 + rt * 128: ct * 1024 + (rt + 1) * 128],
                                    P_.d("xsbf")[tc * 1024 + rt * 128: tc * 1024 + (rt + 1) * 128,
                                                 ct * 128:(ct + 1) * 128], transpose=True)
                        P_.sd.wait(e)
                c1b, e6b, e7b, sgb2 = P_.c1.count, P_.e[6].count, P_.e[7].count, P_.sg.count
                w1_marks = []
                with nc.Block() as blk:
                    @blk.gpsimd
                    def _(e, li=li):
                        for r in range(11):
                            wsl = r % 2
                            if r >= 2:
                                e.wait_ge(P_.c1.h, c1b + 4 * (r - 1))
                            dma(P_, e,
                                w1b[:, :].rearrange("p (s k c) -> p s k c", s=2, k=8)
                                [:, wsl, :, :],
                                P_.d("W1P")[li].rearrange("(k p) c -> p k c", p=128)
                                [:, :, r * 512:(r + 1) * 512], sem=P_.sg)
                            w1_marks.append(P_.sg.count)
                    @blk.tensor
                    def _(e):
                        for r in range(11):
                            wsl = r % 2
                            e.wait_ge(P_.sg.h, w1_marks[r])
                            for u in range(4):
                                pc, ts2 = u // 2, u % 2
                                sl = u % 2
                                if r > 0 or u >= 2:
                                    g_ = 4 * r + u
                                    e.wait_ge(P_.e[7].h, e7b + (g_ - 1))
                                for kc in range(8):
                                    e.matmul(ps_a[:, sl * 512:(sl + 1) * 512],
                                             w1b[:, wsl * 4096 + kc * 512 + (2 * pc) * 128:
                                                 wsl * 4096 + kc * 512 + (2 * pc + 1) * 128],
                                             xt[:, kc * 1024 + ts2 * 512: kc * 1024 + (ts2 + 1) * 512],
                                             start=(kc == 0), stop=(kc == 7))
                                for kc in range(8):
                                    mm = e.matmul(ps_g[:, sl * 512:(sl + 1) * 512],
                                                  w1b[:, wsl * 4096 + kc * 512 + (2 * pc + 1) * 128:
                                                      wsl * 4096 + kc * 512 + (2 * pc + 2) * 128],
                                                  xt[:, kc * 1024 + ts2 * 512: kc * 1024 + (ts2 + 1) * 512],
                                                  start=(kc == 0), stop=(kc == 7))
                                mm.then_inc(P_.c1.h, 1)
                    @blk.scalar
                    def _(e):
                        for r in range(11):
                            for u in range(4):
                                pc = u // 2
                                c = 2 * r + pc
                                sl = u % 2
                                g_ = 4 * r + u
                                e.wait_ge(P_.c1.h, c1b + g_ + 1)
                                if g_ >= 2:
                                    e.wait_ge(P_.e[7].h, e7b + (g_ - 1))
                                act = e.activation(gel[:, sl * 512:(sl + 1) * 512],
                                                   ps_g[:, sl * 512:(sl + 1) * 512], AF.Gelu,
                                                   bias=b1p[:, 2 * c + 1: 2 * c + 2], scale=1.0)
                                act.then_inc(P_.e[6].h, 1)
                    @blk.vector
                    def _(e):
                        for r in range(11):
                            for u in range(4):
                                pc, ts2 = u // 2, u % 2
                                c = 2 * r + pc
                                sl = u % 2
                                g_ = 4 * r + u
                                e.wait_ge(P_.e[6].h, e6b + g_ + 1)
                                i = e.scalar_tensor_tensor(
                                    hbuf[:, c * 1024 + ts2 * 512: c * 1024 + (ts2 + 1) * 512],
                                    ps_a[:, sl * 512:(sl + 1) * 512],
                                    b1p[:, 2 * c: 2 * c + 1],
                                    gel[:, sl * 512:(sl + 1) * 512],
                                    ALU.add, ALU.mult)
                                i.then_inc(P_.e[7].h, 1)
                P_.c1.count = c1b + 44
                P_.e[6].count = e6b + 44
                P_.e[7].count = e7b + 44
                P_.sg.count = sgb2 + 16 * 11
                c1b, c4b, sdb2, ssb2 = P_.c1.count, P_.c4.count, P_.sd.count, P_.ss.count
                with nc.Block() as blk:
                    @blk.sync
                    def _(e, tc=tc):
                        for t2 in range(8):
                            sl = t2 % 2
                            r0 = tc * 1024 + t2 * 128
                            if t2 >= 2:
                                e.wait_ge(P_.c4.h, c4b + (t2 - 1))
                            dma(P_, e, xit[:, sl * 1024:(sl + 1) * 1024],
                                P_.d("xi")[r0:r0 + 128, :], sem=P_.ss)
                            e.wait_ge(P_.c4.h, c4b + t2 + 1)
                            dma(P_, e, P_.d("xi")[r0:r0 + 128, :], ev[:, sl * 1024:(sl + 1) * 1024])
                        P_.sd.wait(e)
                        P_.ss.wait(e)
                    @blk.tensor
                    def _(e):
                        for t2 in range(8):
                            sl = t2 % 2
                            if t2 >= 2:
                                e.wait_ge(P_.c4.h, c4b + (t2 - 1))
                            for u in range(22):
                                for nh in range(2):
                                    e.matmul(ps_y[:, sl * 1024 + nh * 512: sl * 1024 + (nh + 1) * 512],
                                             hbuf[:, u * 1024 + t2 * 128: u * 1024 + t2 * 128 + 128],
                                             w2pb[:, u * 1024 + nh * 512: u * 1024 + (nh + 1) * 512],
                                             start=(u == 0), stop=False)
                            for nh in range(2):
                                mm = e.matmul(ps_y[:, sl * 1024 + nh * 512: sl * 1024 + (nh + 1) * 512],
                                              ones_bf[:, :], b2r[:, nh * 512:(nh + 1) * 512],
                                              start=False, stop=True)
                            mm.then_inc(P_.c1.h, 1)
                    @blk.vector
                    def _(e):
                        for t2 in range(8):
                            sl = t2 % 2
                            e.wait_ge(P_.c1.h, c1b + t2 + 1)
                            e.wait_ge(P_.ss.h, ssb2 + 16 * (t2 + 1))
                            if t2 >= 2:
                                e.wait_ge(P_.sd.h, sdb2 + 16 * (t2 - 1))
                            i = e.tensor_tensor(ev[:, sl * 1024:(sl + 1) * 1024],
                                                ps_y[:, sl * 1024:(sl + 1) * 1024],
                                                xit[:, sl * 1024:(sl + 1) * 1024], ALU.add)
                            i.then_inc(P_.c4.h, 1)
                P_.c1.count = c1b + 8
                P_.c4.count = c4b + 8


def emit_copy(P_, src, dst):
    nc = P_.nc
    with ExitStack() as st:
        buf = st.enter_context(nc.sbuf_tensor(P_.uname("cp"), [128, 4 * 1024], F32))
        for g0 in range(0, NT, 4):
            with nc.Block() as blk:
                @blk.sync
                def _(e, g0=g0):
                    for j in range(4):
                        tt = g0 + j
                        dma(P_, e, buf[:, j * 1024:(j + 1) * 1024],
                            P_.d(src)[tt * 128:(tt + 1) * 128, :])
                    P_.sd.wait(e)
            with nc.Block() as blk:
                @blk.sync
                def _(e, g0=g0):
                    for j in range(4):
                        tt = g0 + j
                        dma(P_, e, P_.d(dst)[tt * 128:(tt + 1) * 128, :],
                            buf[:, j * 1024:(j + 1) * 1024])
                    P_.sd.wait(e)

def emit_add3(P_, a, b, c, dst):
    nc = P_.nc
    with ExitStack() as st:
        buf = st.enter_context(nc.sbuf_tensor(P_.uname("ad"), [128, 3 * 2 * 1024], F32))
        ob = st.enter_context(nc.sbuf_tensor(P_.uname("ao"), [128, 2 * 1024], F32))
        for g0 in range(0, NT, 2):
            with nc.Block() as blk:
                @blk.sync
                def _(e, g0=g0):
                    for j in range(2):
                        tt = g0 + j
                        for si, sn in enumerate((a, b, c)):
                            dma(P_, e, buf[:, (si * 2 + j) * 1024:(si * 2 + j + 1) * 1024],
                                P_.d(sn)[tt * 128:(tt + 1) * 128, :])
                    P_.sd.wait(e)
            with nc.Block() as blk:
                @blk.vector
                def _(e):
                    i = e.tensor_tensor(ob[:, :], buf[:, 0:2048], buf[:, 2048:4096], ALU.add)
                    FV(P_, e, i)
                    e.tensor_tensor(ob[:, :], ob[:, :], buf[:, 4096:6144], ALU.add)
            with nc.Block() as blk:
                @blk.sync
                def _(e, g0=g0):
                    for j in range(2):
                        tt = g0 + j
                        dma(P_, e, P_.d(dst)[tt * 128:(tt + 1) * 128, :],
                            ob[:, j * 1024:(j + 1) * 1024])
                    P_.sd.wait(e)



L0W = ["L0_attn_in_w", "L0_attn_in_b", "L0_attn_conv_w", "L0_attn_conv_b",
       "L0_attn_xp_w", "L0_attn_xp_b", "L0_attn_out_w", "L0_attn_out_b"]
L1FW = ["L1_ffn_Wqkv", "L1_ffn_q_scale", "L1_ffn_k_scale", "L1_ffn_Wo",
        "L1_ffn_ln_g", "L1_ffn_ln_b", "L1_ffn_b2", "W1P", "B1P", "W2P"]
CNS_FFN = ["c_csq", "c_snq", "c_csk", "c_snk", "c_mask0", "c_mask1", "c_mask2", "c_ident"]


def _run(Pobj, core_arrs, outs):
    res = run_bass_kernel_spmd(Pobj.nc, core_arrs, core_ids=list(range(len(core_arrs))))
    Pobj.close()
    return [{k: np.asarray(r[k]) for k in outs} for r in res.results]


def kernel_multi(**inputs):
    consts = build_constants()
    W1P, B1P = permute_w1(np.asarray(inputs["L1_ffn_W1"]), np.asarray(inputs["L1_ffn_b1"]))
    W2P = pad_w2(np.asarray(inputs["L1_ffn_W2"]))
    base = {k: np.ascontiguousarray(np.asarray(v)) for k, v in inputs.items()
            if k not in ("L1_ffn_W1", "L1_ffn_b1", "L1_ffn_W2")}
    base["W1P"], base["B1P"], base["W2P"] = W1P, B1P, W2P
    base.update(consts)
    x = np.asarray(inputs["L0_attn_x"])
    xs = [np.ascontiguousarray(x[2 * c:2 * c + 2].reshape(T, D)) for c in range(NCORES)]
    NC_ = NCORES

    def launch(emit_fns, ins, outs, scratch, per_core):
        P_ = P(inputs=ins, outputs=outs, scratch=scratch)
        for fn in emit_fns:
            fn(P_)
        maps = []
        for c in range(NC_):
            m = {}
            for n in ins:
                m[n] = per_core[n][c] if n in per_core else base[n]
            maps.append(m)
        return _run(P_, maps, outs)

    pc = {"x0": xs}
    r = launch([lambda P_: emit_ln_simple(P_, "x0", False)], ["x0"], ["xnbf"], [], pc)
    pc["xnbf"] = [r[c]["xnbf"] for c in range(NC_)]
    r = launch([emit_l0_attn], ["xnbf", "c_ident"] + L0W, ["aout"], [], pc)
    pc["aout"] = [r[c]["aout"] for c in range(NC_)]
    r = launch([emit_moe_gate], ["xnbf", "c_ident", "L0_ffn_router_w"], ["gdram"], [], pc)
    pc["gdram"] = [r[c]["gdram"] for c in range(NC_)]
    r = launch([emit_moe], ["xnbf", "gdram", "L0_ffn_w13", "L0_ffn_w2"], ["fout"], [], pc)
    pc["fout"] = [r[c]["fout"] for c in range(NC_)]
    r = launch([lambda P_: emit_add3(P_, "x0", "aout", "fout", "x1")],
               ["x0", "aout", "fout"], ["x1"], [], pc)
    pc["x1"] = [r[c]["x1"] for c in range(NC_)]
    r = launch([lambda P_: emit_ln_simple(P_, "x1", True)], ["x1"], ["xnbf", "xnf32"], [], pc)
    pc["xnbf"] = [r[c]["xnbf"] for c in range(NC_)]
    pc["xi"] = [r[c]["xnf32"] for c in range(NC_)]
    r = launch([emit_l1_qkv], ["xnbf", "L1_attn_Wq", "L1_attn_Wk", "L1_attn_Wv"],
               ["qfm", "kfm", "vtm"], [], pc)
    for n in ("qfm", "kfm", "vtm"):
        pc[n] = [r[c][n] for c in range(NC_)]
    r = launch([emit_l1_bias], ["c_onehotT", "L1_attn_rel_bias"], ["STC"], ["ebdP"], pc)
    pc["STC"] = [r[c]["STC"] for c in range(NC_)]
    r = launch([emit_l1_attn], ["qfm", "kfm", "vtm", "STC", "c_ident", "L1_attn_Wo"],
               ["aout"], ["ctxfm"], pc)
    pc["aout"] = [r[c]["aout"] for c in range(NC_)]
    r = launch([lambda P_: emit_l1_ffn(P_), lambda P_: emit_copy(P_, "xi", "fout")],
               ["xi"] + L1FW + CNS_FFN, ["fout"],
               ["xsbf", "qtm", "ktm", "vtm", "ctxfm"], pc)
    pc["fout"] = [r[c]["fout"] for c in range(NC_)]
    r = launch([lambda P_: emit_add3(P_, "x1", "aout", "fout", "OUT")],
               ["x1", "aout", "fout"], ["OUT"], [], pc)
    out = np.stack([r[c]["OUT"].reshape(2, S, D) for c in range(NC_)])
    return out.reshape(B, S, D)


ALL_W = (L0W + ["L0_ffn_router_w", "L0_ffn_w13", "L0_ffn_w2",
                "L1_attn_Wq", "L1_attn_Wk", "L1_attn_Wv", "L1_attn_Wo",
                "L1_attn_rel_bias"] + L1FW)
ALL_C = CNS_FFN + ["c_onehotT"]
SCRATCH = ["xnbf", "xnf32", "aout", "fout", "x1", "gdram", "ebdP", "STC",
           "qfm", "kfm", "vtm", "xi", "xsbf", "qtm", "ktm", "ctxfm"]


def build_fused():
    P_ = P(inputs=["x0"] + ALL_W + ALL_C, outputs=["OUT"], scratch=SCRATCH)
    emit_ln_simple(P_, "x0", False)
    emit_l0_attn(P_)
    emit_moe_gate(P_)
    emit_moe(P_)
    emit_add3(P_, "x0", "aout", "fout", "x1")
    emit_ln_simple(P_, "x1", True, dst_bf="xnbf", dst_f32="xi")
    emit_l1_qkv(P_)
    emit_l1_bias(P_)
    emit_l1_attn(P_)
    emit_l1_ffn(P_)
    emit_add3(P_, "x1", "aout", "xi", "OUT")
    return P_


def kernel(**inputs):
    consts = build_constants()
    W1P, B1P = permute_w1(np.asarray(inputs["L1_ffn_W1"]), np.asarray(inputs["L1_ffn_b1"]))
    W2P = pad_w2(np.asarray(inputs["L1_ffn_W2"]))
    base = {k: np.ascontiguousarray(np.asarray(v)) for k, v in inputs.items()
            if k not in ("L1_ffn_W1", "L1_ffn_b1", "L1_ffn_W2")}
    base["W1P"], base["B1P"], base["W2P"] = W1P, B1P, W2P
    base.update(consts)
    x = np.asarray(inputs["L0_attn_x"])
    P_ = build_fused()
    maps = []
    for c in range(NCORES):
        m = {n: base[n] for n in ALL_W + ALL_C}
        m["x0"] = np.ascontiguousarray(x[2 * c:2 * c + 2].reshape(T, D))
        maps.append(m)
    res = run_bass_kernel_spmd(P_.nc, maps, core_ids=list(range(NCORES)))
    P_.close()
    out = np.stack([np.asarray(r["OUT"]).reshape(2, S, D) for r in res.results])
    return out.reshape(B, S, D)
